# version 02
# baseline (speedup 1.0000x reference)
import math
import contextlib
import numpy as np
import ml_dtypes
import concourse.bass as bass
import concourse.mybir as mybir
from concourse.bass_utils import run_bass_kernel_spmd

F32 = mybir.dt.float32
BF16 = mybir.dt.bfloat16
AF = mybir.ActivationFunctionType
ALU = mybir.AluOpType
AX = mybir.AxisListType
BFnp = ml_dtypes.bfloat16

N_CORES = 8
SEQ = 2048
NSEQ = 2
T = SEQ * NSEQ
D = 1024
NC = D // 128
BLK = 512
NBLK = SEQ // BLK
H = 16
HS = 64
FUT = 4
EPS = 1.0e-5
NEG = -1.0e30


ENGINES = ("T", "V", "S", "G", "Q")
DMA_ENGINES = ("G", "Q")


class _Op:
    __slots__ = ("emit", "waits", "inc", "is_dma", "tag", "dkey")

    def __init__(self, emit, tag):
        self.emit = emit
        self.waits = {}
        self.inc = False
        self.is_dma = False
        self.dkey = None
        self.tag = tag


class Builder:
    def __init__(self):
        self.ops = {e: [] for e in ENGINES}
        self.last_writer = {}
        self.readers = {}

    def _dep(self, op, eng, e2, i2):
        if e2 == eng and i2 >= len(self.ops[e2]):
            return
        if e2 == eng and not self.ops[e2][i2].is_dma:
            return
        if op.waits.get(e2, -1) < i2:
            op.waits[e2] = i2
        self.ops[e2][i2].inc = True

    def op(self, eng, emit, reads=(), writes=(), dma=False, tag=""):
        idx = len(self.ops[eng])
        o = _Op(emit, tag)
        o.is_dma = dma
        for r in reads:
            w = self.last_writer.get(r)
            if w is not None:
                self._dep(o, eng, *w)
            self.readers.setdefault(r, []).append((eng, idx))
        for r in writes:
            w = self.last_writer.get(r)
            if w is not None:
                self._dep(o, eng, *w)
            for rd in self.readers.get(r, ()):
                self._dep(o, eng, *rd)
            self.last_writer[r] = (eng, idx)
            self.readers[r] = []
        self.ops[eng].append(o)
        return (eng, idx)

    def dma(self, queue, out, in_, reads=(), writes=(), tag="", slow=False):
        assert queue in DMA_ENGINES

        def emit(eng):
            return eng.dma_start(out=out, in_=in_,
                                 allow_slow_non_contiguous=slow)

        eng, idx = self.op(queue, emit, reads=reads, writes=writes, dma=True,
                           tag=tag)
        self.ops[eng][idx].dkey = writes[0] if writes else f"_dma{eng}"
        return (eng, idx)

    def barrier(self):
        marks = {}
        for e in ENGINES:
            if self.ops[e]:
                i = len(self.ops[e]) - 1
                self.ops[e][i].inc = True
                marks[e] = i
        for e in ENGINES:
            o = _Op(None, "barrier")
            for e2, i2 in marks.items():
                if e2 != e:
                    o.waits[e2] = i2
            self.ops[e].append(o)

    def finalize(self, nc, stack):
        sems = {e: stack.enter_context(nc.semaphore(f"sem_{e}")) for e in ENGINES}
        dsems = {}
        val = {}
        dcnt = {}
        for e in ENGINES:
            c = 0
            for i, o in enumerate(self.ops[e]):
                if o.is_dma:
                    o.inc = True
                    if o.dkey not in dsems:
                        dsems[o.dkey] = stack.enter_context(
                            nc.semaphore(f"dsem_{len(dsems)}"))
                    dcnt[o.dkey] = dcnt.get(o.dkey, 0) + 16
                    val[(e, i)] = ("D", o.dkey, dcnt[o.dkey])
                else:
                    if o.inc:
                        c += 1
                    val[(e, i)] = ("E", e, c)

        def replay(e, eng):
            sem = sems[e]
            waited = {}
            c = 0
            qfinal = []
            for i, o in enumerate(self.ops[e]):
                for e2, i2 in sorted(o.waits.items()):
                    kind, key, v = val[(e2, i2)]
                    if v > waited.get((kind, key), 0):
                        eng.wait_ge(dsems[key] if kind == "D" else sems[key], v)
                        waited[(kind, key)] = v
                if o.emit is None:
                    continue
                inst = o.emit(eng)
                if o.is_dma:
                    inst.then_inc(dsems[o.dkey], 16)
                elif o.inc:
                    inst.then_inc(sem, 1)
                    c += 1
            if e == "Q":
                for key, tot in dcnt.items():
                    if waited.get(("D", key), 0) < tot:
                        eng.wait_ge(dsems[key], tot)

        blk = stack.enter_context(nc.Block())
        blk.tensor(lambda t: replay("T", t))
        blk.vector(lambda v: replay("V", v))
        blk.scalar(lambda s: replay("S", s))
        blk.gpsimd(lambda g: replay("G", g))
        blk.sync(lambda q: replay("Q", q))
        n = {e: len(self.ops[e]) for e in ENGINES}
        return n



def _t(a):
    return np.ascontiguousarray(np.asarray(a).T.astype(np.float32))


def _chunkvec(v):
    v = np.asarray(v, np.float32)
    n = v.shape[0] // 128
    return np.ascontiguousarray(v.reshape(n, 128).T)


def host_prep(inputs):
    d = {}
    d["wqkvfT"] = _t(inputs["L0_attn_W_qkvf"])
    wc = np.asarray(inputs["L0_attn_W_conv"], np.float32).transpose(0, 2, 1)
    wc = wc.reshape(64, 256)
    d["wconv_c"] = np.ascontiguousarray(
        np.concatenate([wc, wc], axis=0)).astype(BFnp)
    d["wprojT0"] = _t(inputs["L0_attn_W_proj"])
    d["wff1"] = np.ascontiguousarray(np.asarray(inputs["L0_attn_W_ff1"], np.float32))
    d["wff2"] = np.ascontiguousarray(np.asarray(inputs["L0_attn_W_ff2"], np.float32))
    d["w1T"] = _t(inputs["L0_ffn_w1_w"])
    d["w2T"] = _t(inputs["L0_ffn_w2_w"])
    d["h1T"] = _t(inputs["L0_ffn_h1_w"])
    d["h2T"] = _t(inputs["L0_ffn_h2_w"])
    d["w3T"] = _t(inputs["L0_ffn_w3_w"])
    d["wattnT"] = _t(inputs["L1_attn_w_attn"])
    d["wprojT1"] = _t(inputs["L1_attn_w_proj"])
    d["cw1"] = np.ascontiguousarray(np.asarray(inputs["L1_ffn_cn_w1"], np.float32))
    d["cw2"] = np.ascontiguousarray(np.asarray(inputs["L1_ffn_cn_w2"], np.float32))
    d["cw3"] = np.ascontiguousarray(np.asarray(inputs["L1_ffn_cn_w3"], np.float32))
    d["cw4"] = np.ascontiguousarray(np.asarray(inputs["L1_ffn_cn_w4"], np.float32))
    d["gw1"] = np.ascontiguousarray(np.asarray(inputs["L1_ffn_gc_w1"], np.float32))
    d["gw2"] = np.ascontiguousarray(np.asarray(inputs["L1_ffn_gc_w2"], np.float32))
    d["gw3"] = np.ascontiguousarray(np.asarray(inputs["L1_ffn_gc_w3"], np.float32))

    cols = []
    _off = [0]
    def addvec(name, v):
        cv = _chunkvec(v)
        VEC_COLS[name] = (_off[0], cv.shape[1])
        _off[0] += cv.shape[1]
        cols.append(cv)
    VEC_COLS.clear()
    addvec("g1", inputs["L0_attn_ln1_g"]); addvec("b1", inputs["L0_attn_ln1_b"])
    addvec("g2", inputs["L0_attn_ln2_g"]); addvec("b2", inputs["L0_attn_ln2_b"])
    addvec("w1s", inputs["L0_ffn_w1_s"]); addvec("w2s", inputs["L0_ffn_w2_s"])
    addvec("h1s", inputs["L0_ffn_h1_s"]); addvec("h2s", inputs["L0_ffn_h2_s"])
    addvec("w3s", inputs["L0_ffn_w3_s"])
    addvec("cb1", inputs["L1_ffn_cn_b1"])
    addvec("cb2", np.concatenate([inputs["L1_ffn_cn_b2"], np.zeros(512, np.float32)]))
    addvec("cb3", np.concatenate([inputs["L1_ffn_cn_b3"], np.zeros(768, np.float32)]))
    addvec("gb1", np.concatenate([inputs["L1_ffn_gc_b1"], np.zeros(512, np.float32)]))
    addvec("gb2", np.concatenate([inputs["L1_ffn_gc_b2"], np.zeros(768, np.float32)]))
    sc = np.zeros(1024, np.float32)
    sc[0] = inputs["L1_ffn_cn_b4"][0]
    sc[128] = inputs["L1_ffn_gc_b3"][0]
    sc[256] = inputs["L1_ffn_adaptive_range"][0]
    sc[384] = 0.1
    addvec("sc", sc)
    d["vecs"] = np.concatenate(cols, axis=1)

    d["iden"] = np.eye(128, dtype=np.float32).astype(BFnp)
    d["idenf"] = np.eye(128, dtype=np.float32)
    d["mneg"] = np.where(np.arange(128)[None, :] <= np.arange(128)[:, None],
                         0.0, NEG).astype(np.float32)
    onesf = np.zeros((65, 128), np.float32)
    onesf[0] = 1.0
    onesf[32] = 1.0
    onesf[64] = 1.0
    d["onesf"] = onesf
    epsc = np.zeros((65, 1), np.float32)
    epsc[0] = epsc[32] = epsc[64] = EPS
    d["epsc"] = epsc
    d["statl"] = np.full((128, 1), 1.0 / D, np.float32).astype(BFnp)
    e2 = np.zeros((64, 32, 128), np.float32)
    efa = np.zeros((128, 32, 64), np.float32)
    for c in range(8):
        for fi in range(4):
            for m in range(128):
                e2[8 * c + 4 * (m // 64) + fi, 4 * c + fi, m] = 1.0
            for k in range(128):
                efa[k, 4 * c + fi, 8 * c + 4 * (k // 64) + fi] = 1.0
    d["e2"] = np.ascontiguousarray(e2.reshape(64, 32 * 128)).astype(BFnp)
    d["efa"] = np.ascontiguousarray(efa.reshape(128, 32 * 64)).astype(BFnp)
    eh = np.zeros((16, 8, 128), np.float32)
    for c in range(8):
        for m in range(128):
            eh[2 * c + m // 64, c, m] = 1.0
    d["eh"] = np.ascontiguousarray(eh.reshape(16, 8 * 128))
    return d


VEC_COLS = {}



def split_x(x):
    out = []
    for c in range(N_CORES):
        s = x[2 * c:2 * c + 2]
        fm = s.reshape(T, D).T
        out.append(np.ascontiguousarray(fm.astype(np.float32)))
    return out


def merge_x(shards):
    out = np.empty((16, SEQ, D), np.float32)
    for c, s in enumerate(shards):
        out[2 * c:2 * c + 2] = s.T.reshape(NSEQ, SEQ, D)
    return out



class TH:
    __slots__ = ("h", "_res")

    def __init__(self, h, name):
        self.h = h
        self._res = name

    def __getitem__(self, idx):
        return self.h[idx]

    def ap(self):
        return self.h.ap()


class Ctx:

    def __init__(self, b, nc, stack):
        self.b = b
        self.nc = nc
        self.stack = stack
        self.t = {}

    def _reg(self, h, name):
        w = TH(h, name)
        self.t[name] = w
        return w

    def sbuf(self, name, shape, dtype):
        return self._reg(self.stack.enter_context(
            self.nc.sbuf_tensor(name, list(shape), dtype)), name)

    def psum(self, name, shape, dtype=F32):
        return self._reg(self.stack.enter_context(
            self.nc.psum_tensor(name, list(shape), dtype)), name)

    def dram_in(self, name, shape, dtype=F32):
        return self._reg(self.nc.dram_tensor(name, list(shape), dtype,
                                             kind="ExternalInput"), name)

    def dram_out(self, name, shape, dtype=F32):
        return self._reg(self.nc.dram_tensor(name, list(shape), dtype,
                                             kind="ExternalOutput"), name)

    def dram_internal(self, name, shape, dtype=F32):
        return self._reg(self.nc.dram_tensor(name, list(shape), dtype,
                                             kind="Internal"), name)

    def debug(self, name, tile, t0, w, nch, dtype):
        if not DEBUG:
            return
        key = "dbg_" + name
        if key not in self.t:
            self.dram_out(key, [nch * 128, T], dtype)
        h = self.t[key]
        dstv = stream_view(h, t0, w)[:, 0:nch, :]
        self.b.dma("Q", dstv, sb3(tile, w, nch), reads=[tile._res],
                   writes=[h._res], tag="dbg")


def vcol(ctx, name, j=0):
    off, n = VEC_COLS[name]
    assert j < n
    return ctx.t["vecs"][:, off + j:off + j + 1]


def vcol_row0(ctx, name, j=0):
    off, n = VEC_COLS[name]
    return ctx.t["vecs"][0:1, off + j:off + j + 1]


def stream_view(hbm, t0, w):
    v = hbm.ap().rearrange("(c p) t -> p c t", p=128)
    return v[:, :, t0:t0 + w]


def sb3(t, w, nch):
    return t[:, 0:nch * w].rearrange("p (c t) -> p c t", c=nch)



def u_load_stream(ctx, dst, hbm, t0, w, nch, cast=False, tag="ld"):
    b = ctx.b
    src = stream_view(hbm, t0, w)[:, 0:nch, :]
    d = sb3(dst, w, nch)
    q = "G" if cast else "Q"
    b.dma(q, d, src, reads=[hbm._res], writes=[dst._res], tag=tag)


def u_store_stream(ctx, src, hbm, t0, w, nch, tag="st"):
    b = ctx.b
    dstv = stream_view(hbm, t0, w)[:, 0:nch, :]
    s = sb3(src, w, nch)
    b.dma("Q", dstv, s, reads=[src._res], writes=[hbm._res], tag=tag)


def mkrows(ctx, prefix, w,
           names=("mu", "m2", "var", "n1", "n0", "i2", "istd", "c0")):
    return {n: ctx.sbuf(f"{prefix}_r_{n}", [1, w], F32) for n in names}


def u_stats(ctx, xbf, sq, w, nch, pstatA, pstatB, R, inner=False):
    b = ctx.b
    statl = ctx.t["statl"]
    epsc = ctx.t["epsc"]
    b.op("S", lambda s, o=sq[:, 0:nch * w], i=xbf[:, 0:nch * w]:
         s.activation(o, i, AF.Square),
         reads=[xbf._res], writes=[sq._res], tag="sq")
    xv = sb3(xbf, w, nch)
    sv = sb3(sq, w, nch)
    for k in range(nch):
        b.op("T", lambda t, p=pstatA[0:1, 0:w], l=statl[:], r=xv[:, k, :],
             st=(k == 0), sp=(k == nch - 1):
             t.matmul(p, l, r, start=st, stop=sp),
             reads=[xbf._res, "statl"], writes=[pstatA._res], tag="mu")
    for k in range(nch):
        b.op("T", lambda t, p=pstatB[0:1, 0:w], l=statl[:], r=sv[:, k, :],
             st=(k == 0), sp=(k == nch - 1):
             t.matmul(p, l, r, start=st, stop=sp),
             reads=[sq._res, "statl"], writes=[pstatB._res], tag="m2")
    mu, m2, var, n1, n0 = R["mu"], R["m2"], R["var"], R["n1"], R["n0"]
    b.op("V", lambda v, o=mu[0:1, 0:w], i=pstatA[0:1, 0:w]: v.tensor_copy(o, i),
         reads=[pstatA._res], writes=[mu._res], tag="evmu")
    b.op("V", lambda v, o=m2[0:1, 0:w], i=pstatB[0:1, 0:w]: v.tensor_copy(o, i),
         reads=[pstatB._res], writes=[m2._res], tag="evm2")
    b.op("V", lambda v, o=var[0:1, 0:w], a=mu[0:1, 0:w]: v.tensor_mul(o, a, a),
         reads=[mu._res], writes=[var._res], tag="mu2")
    b.op("V", lambda v, o=var[0:1, 0:w], a=m2[0:1, 0:w], c=var[0:1, 0:w]:
         v.tensor_sub(o, a, c), reads=[m2._res, var._res], writes=[var._res], tag="var")
    b.op("S", lambda s, o=n1[0:1, 0:w], i=var[0:1, 0:w]:
         s.activation(o, i, AF.Ln, bias=epsc[0:1, :]),
         reads=[var._res, "epsc"], writes=[n1._res], tag="ln")
    b.op("S", lambda s, o=n1[0:1, 0:w], i=n1[0:1, 0:w]:
         s.activation(o, i, AF.Exp, scale=-0.5),
         reads=[n1._res], writes=[n1._res], tag="exp")
    if inner:
        i2, istd = R["i2"], R["istd"]
        b.op("V", lambda v, o=i2[0:1, 0:w], i=var[0:1, 0:w]:
             v.tensor_scalar_add(o, i, EPS), reads=[var._res], writes=[i2._res])
        b.op("V", lambda v, o=i2[0:1, 0:w], i=i2[0:1, 0:w]:
             v.reciprocal(o, i), reads=[i2._res], writes=[i2._res])
        b.op("V", lambda v, o=i2[0:1, 0:w], a=var[0:1, 0:w], c=i2[0:1, 0:w]:
             v.tensor_mul(o, a, c), reads=[var._res, i2._res], writes=[i2._res])
        b.op("S", lambda s, o=i2[0:1, 0:w], i=i2[0:1, 0:w]:
             s.activation(o, i, AF.Ln, bias=epsc[0:1, :]),
             reads=[i2._res, "epsc"], writes=[i2._res])
        b.op("S", lambda s, o=i2[0:1, 0:w], i=i2[0:1, 0:w]:
             s.activation(o, i, AF.Exp, scale=-0.5),
             reads=[i2._res], writes=[i2._res])
        b.op("S", lambda s, o=istd[0:1, 0:w], i=n1[0:1, 0:w]:
             s.activation(o, i, AF.Copy), reads=[n1._res], writes=[istd._res])
        c0 = R["c0"]
        b.op("V", lambda v, o=c0[0:1, 0:w], i=mu[0:1, 0:w], c=istd[0:1, 0:w]:
             v.scalar_tensor_tensor(o, i, -1.0, c, ALU.mult, ALU.mult),
             reads=[mu._res, istd._res], writes=[c0._res], tag="c0")
        b.op("V", lambda v, o=n1[0:1, 0:w], a=n1[0:1, 0:w], c=i2[0:1, 0:w]:
             v.tensor_mul(o, a, c), reads=[i2._res], writes=[n1._res])
    b.op("V", lambda v, o=n0[0:1, 0:w], i=mu[0:1, 0:w], c=n1[0:1, 0:w]:
         v.scalar_tensor_tensor(o, i, -1.0, c, ALU.mult, ALU.mult),
         reads=[mu._res, n1._res], writes=[n0._res], tag="n0")


def u_bcast_row(ctx, row, w, pbank):
    b = ctx.b
    onesf = ctx.t["onesf"]
    b.op("T", lambda t, p=pbank[:, 0:w], l=onesf[0:1, :], r=row[0:1, 0:w]:
         t.matmul(p, l, r, start=True, stop=True),
         reads=[row._res, "onesf"], writes=[pbank._res], tag="bcr")


def u_bcast2(ctx, R, w, nb, pbank0, pbank1):
    b = ctx.b
    u_bcast_row(ctx, R["n1"], w, pbank0)
    u_bcast_row(ctx, R["n0"], w, pbank1)
    b.op("S", lambda s, o=nb[:, 0:w], i=pbank0[:, 0:w]: s.activation(o, i, AF.Copy),
         reads=[pbank0._res], writes=[nb._res], tag="bcc1")
    b.op("S", lambda s, o=nb[:, w:2 * w], i=pbank1[:, 0:w]: s.activation(o, i, AF.Copy),
         reads=[pbank1._res], writes=[nb._res], tag="bcc0")


def u_ln_apply(ctx, x, nb, w, nch, out, gname=None, bname=None, x_reads=None):
    b = ctx.b
    xv = sb3(x, w, nch)
    ov = sb3(out, w, nch)
    tmp = ctx.t["lntmp"]
    for c in range(nch):
        rd = [x._res, nb._res] if x_reads is None else x_reads + [nb._res]
        b.op("V", lambda v, o=tmp[:, 0:w], a=xv[:, c, :], n=nb[:, 0:w]:
             v.tensor_mul(o, a, n), reads=rd, writes=[tmp._res], tag="lnm")
        if gname is None:
            b.op("V", lambda v, o=ov[:, c, :], a=tmp[:, 0:w], n=nb[:, w:2 * w]:
                 v.tensor_add(o, a, n), reads=[tmp._res, nb._res],
                 writes=[out._res], tag="lna")
        else:
            b.op("V", lambda v, o=tmp[:, 0:w], a=tmp[:, 0:w], n=nb[:, w:2 * w]:
                 v.tensor_add(o, a, n), reads=[tmp._res, nb._res],
                 writes=[tmp._res], tag="lna")
            b.op("V", lambda v, o=ov[:, c, :], i=tmp[:, 0:w],
                 g=vcol(ctx, gname, c), bb=vcol(ctx, bname, c):
                 v.tensor_scalar(o, i, g, bb, ALU.mult, ALU.add),
                 reads=[tmp._res, "vecs"], writes=[out._res], tag="lngb")


def u_wload(ctx, dst, hbm, kchunks, c0, c1, tag="wld", nsplit=2):
    b = ctx.b
    w = c1 - c0
    src = hbm.ap().rearrange("(kc p) o -> p kc o", p=128)
    d = dst[:, 0:kchunks * w].rearrange("p (k o) -> p k o", k=kchunks)
    step = max(1, kchunks // nsplit)
    for k0 in range(0, kchunks, step):
        k1 = min(kchunks, k0 + step)
        b.dma("G", d[:, k0:k1, :], src[:, k0:k1, c0:c1],
              reads=[hbm._res], writes=[dst._res], tag=tag, slow=(w < 16))


def wtile(wsb, k, m, kw):
    off, width = m
    return wsb[:, k * kw + off: k * kw + off + width]



def emit_A2a(ctx):
    b = ctx.b
    xa1 = ctx.t["xa1"]
    hscr = ctx.t["hscr"]
    wsb = ctx.sbuf("wff1_sb", [128, 8 * 4096], BF16)
    u_wload(ctx, wsb, ctx.t["wff1"], 8, 0, 4096, nsplit=4)
    xbf = ctx.sbuf("a2a_xbf", [128, NC * BLK], BF16)
    sq = ctx.sbuf("a2a_sq", [128, NC * BLK], BF16)
    u2 = ctx.sbuf("a2a_u2", [128, NC * BLK], BF16)
    nb = ctx.sbuf("a2a_nb", [128, 2 * BLK], F32)
    R = mkrows(ctx, "a2a", BLK)
    ctx.sbuf("lntmp", [128, BLK], F32)
    hstg = ctx.sbuf("a2a_hstg", [128, 4 * BLK], BF16)
    pstat = ctx.psum("a2a_pstat", [1, BLK])
    pstatB = ctx.psum("a2a_pstatB", [1, BLK])
    pbb = [ctx.psum(f"a2a_pbb{i}", [128, BLK]) for i in range(2)]
    pb = [ctx.psum(f"a2a_pb{i}", [128, BLK]) for i in range(4)]
    hview = hscr.ap().rearrange("(c p) t -> p c t", p=128)
    u2v = sb3(u2, BLK, NC)

    for blk_i in range(T // BLK):
        t0 = blk_i * BLK
        u_load_stream(ctx, xbf, xa1, t0, BLK, NC, cast=True, tag="xa1ld")
        u_stats(ctx, xbf, sq, BLK, NC, pstat, pstatB, R, inner=False)
        u_bcast2(ctx, R, BLK, nb, pbb[0], pbb[1])
        u_ln_apply(ctx, xbf, nb, BLK, NC, u2, gname="g2", bname="b2")
        ctx.debug("u2", u2, t0, BLK, NC, BF16)
        ctx.debug("xbf", xbf, t0, BLK, NC, BF16)
        ctx.debug("nb", nb, t0, BLK, 2, F32)
        for m in range(32):
            p = pb[m % 4]
            for k in range(NC):
                b.op("T", lambda t, pp=p[:, 0:BLK],
                     l=wtile(wsb, k, (m * 128, 128), 4096), r=u2v[:, k, :],
                     st=(k == 0), sp=(k == NC - 1):
                     t.matmul(pp, l, r, start=st, stop=sp),
                     reads=[wsb._res, u2._res], writes=[p._res], tag="ff1")
            sl = hstg[:, (m % 4) * BLK:(m % 4 + 1) * BLK]
            b.op("S", lambda s, o=sl, i=p[:, 0:BLK]:
                 s.activation(o, i, AF.Gelu),
                 reads=[p._res], writes=[hstg._res], tag="gelu")
            b.dma("Q", hview[:, m, t0:t0 + BLK], sl,
                  reads=[hstg._res], writes=[hscr._res], tag="hst")



def emit_A2b(ctx):
    b = ctx.b
    xa1, x0, x1, hscr = ctx.t["xa1"], ctx.t["x0"], ctx.t["x1"], ctx.t["hscr"]
    wsb = ctx.sbuf("wff2_sb", [128, 32 * 1024], BF16)
    u_wload(ctx, wsb, ctx.t["wff2"], 32, 0, 1024, nsplit=8)
    hblk = ctx.sbuf("a2b_h", [128, 32 * BLK], BF16)
    xa1b = ctx.sbuf("a2b_xa1", [128, NC * BLK], F32)
    x0b = ctx.sbuf("a2b_x0", [128, NC * BLK], F32)
    xa2 = ctx.sbuf("a2b_xa2", [128, NC * BLK], F32)
    xbf = ctx.sbuf("a2b_xbf", [128, NC * BLK], BF16)
    sq = ctx.sbuf("a2b_sq", [128, NC * BLK], BF16)
    nb = ctx.sbuf("a2b_nb", [128, 2 * BLK], F32)
    R = mkrows(ctx, "a2b", BLK)
    ctx.sbuf("lntmp", [128, BLK], F32)
    stg = ctx.sbuf("a2b_stg", [128, NC * BLK], F32)
    pstat = ctx.psum("a2b_pstat", [1, BLK])
    pstatB = ctx.psum("a2b_pstatB", [1, BLK])
    pbb = [ctx.psum(f"a2b_pbb{i}", [128, BLK]) for i in range(2)]
    pb = [ctx.psum(f"a2b_pb{i}", [128, BLK]) for i in range(4)]
    hview = hscr.ap().rearrange("(c p) t -> p c t", p=128)
    hbv = sb3(hblk, BLK, 32)
    xa1v = sb3(xa1b, BLK, NC)
    xa2v = sb3(xa2, BLK, NC)
    x0v = sb3(x0b, BLK, NC)
    stgv = sb3(stg, BLK, NC)

    for blk_i in range(T // BLK):
        t0 = blk_i * BLK
        for k0 in range(0, 32, 8):
            b.dma("Q", hbv[:, k0:k0 + 8, :], hview[:, k0:k0 + 8, t0:t0 + BLK],
                  reads=[hscr._res], writes=[hblk._res], tag="hld")
        u_load_stream(ctx, xa1b, xa1, t0, BLK, NC, tag="xa1ld")
        u_load_stream(ctx, x0b, x0, t0, BLK, NC, tag="x0ld")
        for m in range(NC):
            p = pb[m % 4]
            for k in range(32):
                b.op("T", lambda t, pp=p[:, 0:BLK],
                     l=wtile(wsb, k, (m * 128, 128), 1024), r=hbv[:, k, :],
                     st=(k == 0), sp=(k == 31):
                     t.matmul(pp, l, r, start=st, stop=sp),
                     reads=[wsb._res, hblk._res], writes=[p._res], tag="ff2")
            b.op("V", lambda v, o=xa2v[:, m, :], a=p[:, 0:BLK], c=xa1v[:, m, :]:
                 v.tensor_add(o, a, c),
                 reads=[p._res, xa1b._res], writes=[xa2._res], tag="xa2")
            b.op("G", lambda g, o=sb3(xbf, BLK, NC)[:, m, :], i=xa2v[:, m, :]:
                 g.tensor_copy(o, i),
                 reads=[xa2._res], writes=[xbf._res], tag="xa2bf")
        u_stats(ctx, xbf, sq, BLK, NC, pstat, pstatB, R, inner=False)
        u_bcast2(ctx, R, BLK, nb, pbb[0], pbb[1])
        for c in range(NC):
            b.op("V", lambda v, o=stgv[:, c, :], a=xa2v[:, c, :], n=nb[:, 0:BLK]:
                 v.tensor_mul(o, a, n),
                 reads=[xa2._res, nb._res], writes=[stg._res], tag="e1")
            b.op("V", lambda v, o=stgv[:, c, :], a=stgv[:, c, :], n=nb[:, BLK:2 * BLK]:
                 v.tensor_add(o, a, n),
                 reads=[stg._res, nb._res], writes=[stg._res], tag="e2")
            b.op("V", lambda v, o=stgv[:, c, :], a=stgv[:, c, :], cc=x0v[:, c, :]:
                 v.tensor_add(o, a, cc),
                 reads=[stg._res, x0b._res], writes=[stg._res], tag="e3")
        u_store_stream(ctx, stg, x1, t0, BLK, NC, tag="x1st")



class Ph:
    def __init__(self, emit, ins, outs, consts=()):
        self.emit = emit
        self.ins = list(ins)
        self.outs = list(outs)
        self.consts = list(consts)


DEBUG = False
LIMIT_BLOCKS = None
BASE_CONSTS = ["vecs", "onesf", "statl", "epsc"]

CONST_SPECS = {
    "onesf": ([65, 128], F32),
    "epsc": ([65, 1], F32),
    "statl": ([128, 1], BF16),
    "iden": ([128, 128], BF16),
    "idenf": ([128, 128], F32),
    "mneg": ([128, 128], F32),
    "e2": ([64, 4096], BF16),
    "eh": ([16, 1024], F32),
    "efa": ([128, 2048], BF16),
    "wconv_c": ([128, 256], BF16),
}

HBM_SPECS = {
    "x0": ([D, T], F32), "xa1": ([D, T], F32), "x1": ([D, T], F32),
    "x2": ([D, T], F32), "x3": ([D, T], F32), "x4": ([D, T], F32),
    "hscr": ([4096, T], BF16), "hscr2": ([4096, T], BF16),
    "wqkvfT": ([1024, 4096], F32), "wprojT0": ([1024, 1024], F32),
    "wff1": ([1024, 4096], F32), "wff2": ([4096, 1024], F32),
    "w1T": ([1024, 4096], F32), "w2T": ([1024, 4096], F32),
    "h1T": ([4096, 4096], F32), "h2T": ([4096, 4096], F32),
    "w3T": ([4096, 1024], F32),
    "wattnT": ([1024, 3072], F32), "wprojT1": ([1024, 1024], F32),
    "cw1": ([1024, 1024], F32), "cw2": ([1024, 512], F32),
    "cw3": ([512, 256], F32), "cw4": ([256, 1], F32),
    "gw1": ([1024, 512], F32), "gw2": ([512, 256], F32), "gw3": ([256, 1], F32),
}


def load_consts(ctx, names):
    b = ctx.b
    for nm in names:
        shape, dt = CONST_SPECS.get(nm, (None, None))
        if nm == "vecs":
            off, n = max(VEC_COLS.values(), key=lambda v: v[0] + v[1])
            shape, dt = [128, off + n], F32
        h = ctx.dram_in(nm + "_h", shape, dt)
        s = ctx.sbuf(nm, shape, dt)
        b.dma("Q", s[:], h.ap(), reads=[h._res], writes=[nm], tag="cst")


def build_program(phase_names):
    nc = bass.Bass("TRN2", target_bir_lowering=False)
    b = Builder()
    with contextlib.ExitStack() as stack:
        ctx = Ctx(b, nc, stack)
        ins, outs, consts = set(), set(), set(BASE_CONSTS)
        for pn in phase_names:
            p = PHASES[pn]
            ins |= set(p.ins)
            outs |= set(p.outs)
            consts |= set(p.consts)
        ins -= outs
        for nm in sorted(ins):
            ctx.dram_in(nm, *_spec(nm))
        for nm in sorted(outs):
            ctx.dram_out(nm, *_spec(nm))
        load_consts(ctx, sorted(consts))
        for i, pn in enumerate(phase_names):
            PHASES[pn].emit(ctx)
            if i + 1 < len(phase_names):
                b.barrier()
        counts = b.finalize(nc, stack)
    return nc, counts


def _spec(nm):
    return HBM_SPECS[nm]


PHASES = {}

LAUNCH_PLAN = ["A1", "A2a", "A2b", "F1", "F2", "F3", "F4", "B1", "C1"]


def _const_arrays(shared, consts):
    m = {}
    for nm in consts:
        key = {"wconv_c": "wconv_c"}.get(nm, nm)
        m[nm + "_h"] = shared[key]
    return m


def kernel(**inputs):
    shared = host_prep(inputs)
    xs = split_x(np.asarray(inputs["L0_attn_x"], np.float32))
    streams = [dict(x0=xs[c]) for c in range(N_CORES)]
    for pn in LAUNCH_PLAN:
        p = PHASES[pn]
        nc, _ = build_program([pn])
        consts = sorted(set(BASE_CONSTS) | set(p.consts))
        in_maps = []
        for c in range(N_CORES):
            m = _const_arrays(shared, consts)
            for nm in p.ins:
                m[nm] = shared[nm] if nm in shared else streams[c][nm]
            in_maps.append(m)
        res = run_bass_kernel_spmd(nc, in_maps, core_ids=list(range(N_CORES)))
        for c in range(N_CORES):
            for nm in p.outs:
                streams[c][nm] = res.results[c][nm]
    return merge_x([s["x4"] for s in streams])



def emit_F1(ctx):
    b = ctx.b
    x1, hscr = ctx.t["x1"], ctx.t["hscr"]
    w1 = ctx.sbuf("w1_sb", [128, 8 * 4096], BF16)
    w2 = ctx.sbuf("w2_sb", [128, 8 * 4096], BF16)
    u_wload(ctx, w1, ctx.t["w1T"], 8, 0, 4096, nsplit=4)
    u_wload(ctx, w2, ctx.t["w2T"], 8, 0, 4096, nsplit=4)
    xbf = ctx.sbuf("f1_xbf", [128, NC * BLK], BF16)
    sq = ctx.sbuf("f1_sq", [128, NC * BLK], BF16)
    u = ctx.sbuf("f1_u", [128, NC * BLK], BF16)
    nb = ctx.sbuf("f1_nb", [128, 2 * BLK], F32)
    R = mkrows(ctx, "f1", BLK)
    ctx.sbuf("lntmp", [128, BLK], F32)
    asb = ctx.sbuf("f1_a", [128, 2 * BLK], BF16)
    hstg = ctx.sbuf("f1_hstg", [128, 4 * BLK], BF16)
    pstat = ctx.psum("f1_pstat", [1, BLK])
    pstatB = ctx.psum("f1_pstatB", [1, BLK])
    pbb = [ctx.psum(f"f1_pbb{i}", [128, BLK]) for i in range(2)]
    pb = [ctx.psum(f"f1_pb{i}", [128, BLK]) for i in range(4)]
    hview = hscr.ap().rearrange("(c p) t -> p c t", p=128)
    uv = sb3(u, BLK, NC)

    for blk_i in range(T // BLK):
        t0 = blk_i * BLK
        u_load_stream(ctx, xbf, x1, t0, BLK, NC, cast=True)
        u_stats(ctx, xbf, sq, BLK, NC, pstat, pstatB, R)
        u_bcast2(ctx, R, BLK, nb, pbb[0], pbb[1])
        u_ln_apply(ctx, xbf, nb, BLK, NC, u)
        for m in range(32):
            p1 = pb[2 * (m % 2)]
            p2 = pb[2 * (m % 2) + 1]
            for k in range(NC):
                b.op("T", lambda t, pp=p1[:, 0:BLK],
                     l=wtile(w1, k, (m * 128, 128), 4096), r=uv[:, k, :],
                     st=(k == 0), sp=(k == NC - 1):
                     t.matmul(pp, l, r, start=st, stop=sp),
                     reads=[w1._res, u._res], writes=[p1._res], tag="t1")
            for k in range(NC):
                b.op("T", lambda t, pp=p2[:, 0:BLK],
                     l=wtile(w2, k, (m * 128, 128), 4096), r=uv[:, k, :],
                     st=(k == 0), sp=(k == NC - 1):
                     t.matmul(pp, l, r, start=st, stop=sp),
                     reads=[w2._res, u._res], writes=[p2._res], tag="t2")
            av = asb[:, (m % 2) * BLK:(m % 2 + 1) * BLK]
            b.op("S", lambda s, o=av, i=p1[:, 0:BLK], sc=vcol(ctx, "w1s", m):
                 s.activation(o, i, AF.Silu, scale=sc),
                 reads=[p1._res, "vecs"], writes=[asb._res], tag="silu")
            sl = hstg[:, (m % 4) * BLK:(m % 4 + 1) * BLK]
            b.op("V", lambda v, o=sl, i=p2[:, 0:BLK], sc=vcol(ctx, "w2s", m), a=av:
                 v.scalar_tensor_tensor(o, i, sc, a, ALU.mult, ALU.mult),
                 reads=[p2._res, asb._res, "vecs"], writes=[hstg._res], tag="hmul")
            b.dma("Q", hview[:, m, t0:t0 + BLK], sl,
                  reads=[hstg._res], writes=[hscr._res], tag="hst")


def _emit_big(ctx, pfx, src_name, dst_name, w_name, sc_name):
    b = ctx.b
    srch, dsth = ctx.t[src_name], ctx.t[dst_name]
    wh = ctx.t[w_name]
    insb = ctx.sbuf(f"{pfx}_in", [128, 32 * SEQ], BF16)
    wst = [ctx.sbuf(f"{pfx}_ws{i}", [128, 32 * 128], BF16) for i in range(2)]
    ostg = ctx.sbuf(f"{pfx}_ostg", [128, 2 * 4 * BLK], BF16)
    pb = [ctx.psum(f"{pfx}_pb{i}", [128, BLK]) for i in range(8)]
    sview = srch.ap().rearrange("(c p) t -> p c t", p=128)
    dview = dsth.ap().rearrange("(c p) t -> p c t", p=128)
    wsrc = wh.ap().rearrange("(kc p) o -> p kc o", p=128)
    inv = sb3(insb, SEQ, 32)

    for s in range(NSEQ):
        c0 = s * SEQ
        for k0 in range(0, 32, 8):
            b.dma("Q", inv[:, k0:k0 + 8, :], sview[:, k0:k0 + 8, c0:c0 + SEQ],
                  reads=[src_name], writes=[insb._res], tag="inld")
        for m in range(32):
            wbuf = wst[m % 2]
            wv = wbuf[:, 0:32 * 128].rearrange("p (k o) -> p k o", k=32)
            for k0 in range(0, 32, 16):
                b.dma("G", wv[:, k0:k0 + 16, :],
                      wsrc[:, k0:k0 + 16, m * 128:(m + 1) * 128],
                      reads=[w_name], writes=[wbuf._res], tag="wst")
            banks = pb[(m % 2) * 4:(m % 2) * 4 + 4]
            for tt in range(4):
                p = banks[tt]
                for k in range(32):
                    b.op("T", lambda t, pp=p[:, 0:BLK],
                         l=wv[:, k, :], r=inv[:, k, tt * BLK:(tt + 1) * BLK],
                         st=(k == 0), sp=(k == 31):
                         t.matmul(pp, l, r, start=st, stop=sp),
                         reads=[wbuf._res, insb._res], writes=[p._res], tag="mm")
            half = (m % 2) * 4 * BLK
            for tt in range(4):
                b.op("S", lambda ss, o=ostg[:, half + tt * BLK:half + (tt + 1) * BLK],
                     i=banks[tt][:, 0:BLK], sc=vcol(ctx, sc_name, m):
                     ss.activation(o, i, AF.Gelu, scale=sc),
                     reads=[banks[tt]._res, "vecs"], writes=[ostg._res], tag="gelu")
            b.dma("Q", dview[:, m, c0:c0 + SEQ], ostg[:, half:half + 4 * BLK],
                  reads=[ostg._res], writes=[dst_name], tag="ost")


def emit_F2(ctx):
    _emit_big(ctx, "f2", "hscr", "hscr2", "h1T", "h1s")


def emit_F3(ctx):
    _emit_big(ctx, "f3", "hscr2", "hscr", "h2T", "h2s")


def emit_F4(ctx):
    b = ctx.b
    hscr, x1, x2 = ctx.t["hscr"], ctx.t["x1"], ctx.t["x2"]
    wsb = ctx.sbuf("w3_sb", [128, 32 * 1024], BF16)
    u_wload(ctx, wsb, ctx.t["w3T"], 32, 0, 1024, nsplit=8)
    hblk = ctx.sbuf("f4_h", [128, 32 * BLK], BF16)
    x1b = ctx.sbuf("f4_x1", [128, NC * BLK], F32)
    osb = ctx.sbuf("f4_o", [128, NC * BLK], F32)
    xbf = ctx.sbuf("f4_xbf", [128, NC * BLK], BF16)
    sq = ctx.sbuf("f4_sq", [128, NC * BLK], BF16)
    nb = ctx.sbuf("f4_nb", [128, 2 * BLK], F32)
    R = mkrows(ctx, "f4", BLK)
    ctx.sbuf("lntmp", [128, BLK], F32)
    stg = ctx.sbuf("f4_stg", [128, NC * BLK], F32)
    pstat = ctx.psum("f4_pstat", [1, BLK])
    pstatB = ctx.psum("f4_pstatB", [1, BLK])
    pbb = [ctx.psum(f"f4_pbb{i}", [128, BLK]) for i in range(2)]
    pb = [ctx.psum(f"f4_pb{i}", [128, BLK]) for i in range(4)]
    hview = hscr.ap().rearrange("(c p) t -> p c t", p=128)
    hbv = sb3(hblk, BLK, 32)
    osv = sb3(osb, BLK, NC)
    x1v = sb3(x1b, BLK, NC)
    stgv = sb3(stg, BLK, NC)

    for blk_i in range(T // BLK):
        t0 = blk_i * BLK
        for k0 in range(0, 32, 8):
            b.dma("Q", hbv[:, k0:k0 + 8, :], hview[:, k0:k0 + 8, t0:t0 + BLK],
                  reads=[hscr._res], writes=[hblk._res], tag="hld")
        u_load_stream(ctx, x1b, x1, t0, BLK, NC, tag="x1ld")
        for m in range(NC):
            p = pb[m % 4]
            for k in range(32):
                b.op("T", lambda t, pp=p[:, 0:BLK],
                     l=wtile(wsb, k, (m * 128, 128), 1024), r=hbv[:, k, :],
                     st=(k == 0), sp=(k == 31):
                     t.matmul(pp, l, r, start=st, stop=sp),
                     reads=[wsb._res, hblk._res], writes=[p._res], tag="w3")
            b.op("S", lambda s, o=osv[:, m, :], i=p[:, 0:BLK], sc=vcol(ctx, "w3s", m):
                 s.activation(o, i, AF.Copy, scale=sc),
                 reads=[p._res, "vecs"], writes=[osb._res], tag="osc")
            b.op("G", lambda g, o=sb3(xbf, BLK, NC)[:, m, :], i=osv[:, m, :]:
                 g.tensor_copy(o, i),
                 reads=[osb._res], writes=[xbf._res], tag="obf")
        u_stats(ctx, xbf, sq, BLK, NC, pstat, pstatB, R)
        u_bcast2(ctx, R, BLK, nb, pbb[0], pbb[1])
        for c in range(NC):
            b.op("V", lambda v, o=stgv[:, c, :], a=osv[:, c, :], n=nb[:, 0:BLK]:
                 v.tensor_mul(o, a, n),
                 reads=[osb._res, nb._res], writes=[stg._res], tag="e1")
            b.op("V", lambda v, o=stgv[:, c, :], a=stgv[:, c, :], n=nb[:, BLK:2 * BLK]:
                 v.tensor_add(o, a, n),
                 reads=[stg._res, nb._res], writes=[stg._res], tag="e2")
            b.op("V", lambda v, o=stgv[:, c, :], a=stgv[:, c, :], cc=x1v[:, c, :]:
                 v.tensor_add(o, a, cc),
                 reads=[stg._res, x1b._res], writes=[stg._res], tag="e3")
        u_store_stream(ctx, stg, x2, t0, BLK, NC, tag="x2st")






def emit_C1(ctx):
    b = ctx.b
    x3, x4 = ctx.t["x3"], ctx.t["x4"]
    cw1 = ctx.sbuf("cw1_sb", [128, 8 * 1024], BF16)
    cw2 = ctx.sbuf("cw2_sb", [128, 8 * 512], BF16)
    cw3 = ctx.sbuf("cw3_sb", [128, 4 * 256], BF16)
    cw4 = ctx.sbuf("cw4_sb", [128, 2 * 1], BF16)
    gw1 = ctx.sbuf("gw1_sb", [128, 8 * 512], BF16)
    gw2 = ctx.sbuf("gw2_sb", [128, 4 * 256], BF16)
    gw3 = ctx.sbuf("gw3_sb", [128, 2 * 1], BF16)
    u_wload(ctx, cw1, ctx.t["cw1"], 8, 0, 1024, nsplit=2)
    u_wload(ctx, cw2, ctx.t["cw2"], 8, 0, 512, nsplit=2)
    u_wload(ctx, cw3, ctx.t["cw3"], 4, 0, 256, nsplit=1)
    u_wload(ctx, cw4, ctx.t["cw4"], 2, 0, 1, nsplit=1)
    u_wload(ctx, gw1, ctx.t["gw1"], 8, 0, 512, nsplit=2)
    u_wload(ctx, gw2, ctx.t["gw2"], 4, 0, 256, nsplit=1)
    u_wload(ctx, gw3, ctx.t["gw3"], 2, 0, 1, nsplit=1)
    outr = ctx.sbuf("c1_out", [128, NC * SEQ], F32)
    x3b = ctx.sbuf("c1_x3", [128, NC * BLK], F32)
    obf = ctx.sbuf("c1_obf", [128, NC * BLK], BF16)
    sq = ctx.sbuf("c1_sq", [128, NC * BLK], BF16)
    h1bf = ctx.sbuf("c1_h1", [128, 8 * BLK], BF16)
    h2bf = ctx.sbuf("c1_h2", [128, 4 * BLK], BF16)
    h3bf = ctx.sbuf("c1_h3", [128, 2 * BLK], BF16)
    cb = ctx.sbuf("c1_cb", [128, 2 * BLK], F32)
    nb = ctx.sbuf("c1_nb", [128, 2 * BLK], F32)
    gc = ctx.sbuf("c1_gc", [128, 16], F32)
    gcp = ctx.sbuf("c1_gcp", [128, 32], F32)
    g12 = ctx.sbuf("c1_g12", [128, 8], BF16)
    coef = ctx.sbuf("c1_coef", [1, 8], F32)
    cpl = ctx.sbuf("c1_cpl", [1, 2 * BLK], F32)
    tA = ctx.sbuf("c1_tA", [128, BLK], F32)
    tB = ctx.sbuf("c1_tB", [128, BLK], F32)
    stg = ctx.sbuf("c1_stg", [128, NC * BLK], F32)
    R = mkrows(ctx, "c1", BLK)
    ctx.sbuf("lntmp", [128, BLK], F32)
    pstat = ctx.psum("c1_pstat", [1, BLK])
    pstatB = ctx.psum("c1_pstatB", [1, BLK])
    pbb = [ctx.psum(f"c1_pbb{i}", [128, BLK]) for i in range(2)]
    pb = [ctx.psum(f"c1_pb{i}", [128, BLK]) for i in range(4)]
    x3v = sb3(x3b, BLK, NC)
    obv = sb3(obf, BLK, NC)
    stgv = sb3(stg, BLK, NC)

    for s in range(NSEQ):
        sc0 = s * SEQ

        for bi in range(NBLK):
            t0 = sc0 + bi * BLK
            lt0 = bi * BLK
            u_load_stream(ctx, x3b, x3, t0, BLK, NC, tag="x3ld")
            for c in range(NC):
                b.op("G", lambda g, o=obv[:, c, :], i=x3v[:, c, :]: g.tensor_copy(o, i),
                     reads=[x3b._res], writes=[obf._res], tag="xbf")
            u_stats(ctx, obf, sq, BLK, NC, pstat, pstatB, R)
            u_bcast2(ctx, R, BLK, nb, pbb[0], pbb[1])
            for c in range(NC):
                ods = outr[:, c * SEQ + lt0:c * SEQ + lt0 + BLK]
                b.op("V", lambda v, o=ods, a=x3v[:, c, :], n=nb[:, 0:BLK]:
                     v.tensor_mul(o, a, n),
                     reads=[x3b._res, nb._res], writes=[outr._res], tag="o1")
                b.op("V", lambda v, o=ods, a=ods, n=nb[:, BLK:2 * BLK]:
                     v.tensor_add(o, a, n),
                     reads=[outr._res, nb._res], writes=[outr._res], tag="o2")
                b.op("V", lambda v, o=gcp[:, c * 4 + bi:c * 4 + bi + 1], i=ods:
                     v.reduce_sum(o, i, axis=AX.X),
                     reads=[outr._res], writes=[gcp._res], tag="gcr")
        if DEBUG:
            ctx.debug("oln", outr, sc0, SEQ, NC, F32)
        for c in range(NC):
            b.op("V", lambda v, o=gc[:, c:c + 1], i=gcp[:, c * 4:(c + 1) * 4]:
                 v.reduce_sum(o, i, axis=AX.X),
                 reads=[gcp._res], writes=[gc._res], tag="gcf")
        gcbf = ctx.t.get("c1_gcbf") or ctx.sbuf("c1_gcbf", [128, 8], BF16)
        for c in range(NC):
            b.op("S", lambda ss, o=gcbf[:, c:c + 1], i=gc[:, c:c + 1]:
                 ss.activation(o, i, AF.Copy, scale=1.0 / SEQ),
                 reads=[gc._res], writes=[gcbf._res], tag="gcb")
        pg = pbb[0]
        for m in range(4):
            for k in range(NC):
                b.op("T", lambda t, pp=pg[:, m:m + 1],
                     l=wtile(gw1, k, (m * 128, 128), 512), r=gcbf[:, k:k + 1],
                     st=(k == 0), sp=(k == NC - 1):
                     t.matmul(pp, l, r, start=st, stop=sp),
                     reads=[gw1._res, gcbf._res], writes=[pg._res], tag="g1")
            b.op("S", lambda ss, o=g12[:, m:m + 1], i=pg[:, m:m + 1],
                 bi=vcol(ctx, "gb1", m):
                 ss.activation(o, i, AF.Gelu, bias=bi),
                 reads=[pg._res, "vecs"], writes=[g12._res], tag="g1g")
        pg2 = pbb[1]
        for m in range(2):
            for k in range(4):
                b.op("T", lambda t, pp=pg2[:, m:m + 1],
                     l=wtile(gw2, k, (m * 128, 128), 256), r=g12[:, k:k + 1],
                     st=(k == 0), sp=(k == 3):
                     t.matmul(pp, l, r, start=st, stop=sp),
                     reads=[gw2._res, g12._res], writes=[pg2._res], tag="g2")
            b.op("S", lambda ss, o=g12[:, 4 + m:5 + m], i=pg2[:, m:m + 1],
                 bi=vcol(ctx, "gb2", m):
                 ss.activation(o, i, AF.Gelu, bias=bi),
                 reads=[pg2._res, "vecs"], writes=[g12._res], tag="g2g")
        for k in range(2):
            b.op("T", lambda t, pp=pstat[0:1, 0:1],
                 l=wtile(gw3, k, (0, 1), 1), r=g12[:, 4 + k:5 + k],
                 st=(k == 0), sp=(k == 1):
                 t.matmul(pp, l, r, start=st, stop=sp),
                 reads=[gw3._res, g12._res], writes=[pstat._res], tag="g3")
        b.op("S", lambda ss, o=coef[0:1, 4:5], i=pstat[0:1, 0:1],
             bi=vcol_row0(ctx, "sc", 1):
             ss.activation(o, i, AF.Sigmoid, bias=bi),
             reads=[pstat._res, "vecs"], writes=[coef._res], tag="gf")
        arv = ctx.t["vecs"][0:1, VEC_COLS["sc"][0] + 2:VEC_COLS["sc"][0] + 3]
        b.op("V", lambda v, o=coef[0:1, 0:1], i=arv:
             v.tensor_scalar_mul(o, i, 0.6), reads=["vecs"], writes=[coef._res])
        b.op("V", lambda v, o=coef[0:1, 2:3], i=arv:
             v.tensor_scalar_mul(o, i, 1.4), reads=["vecs"], writes=[coef._res])
        b.op("V", lambda v, o=coef[0:1, 3:4], i=arv:
             v.tensor_scalar(o, i, -1.0, 0.1, ALU.mult, ALU.add),
             reads=["vecs"], writes=[coef._res])
        b.op("V", lambda v, o=coef[0:1, 1:2], i=coef[0:1, 4:5],
             s1=coef[0:1, 2:3], s2=coef[0:1, 3:4]:
             v.tensor_scalar(o, i, s1, s2, ALU.mult, ALU.add),
             reads=[coef._res], writes=[coef._res])
        if DEBUG:
            if "dbg_coef" not in ctx.t:
                ctx.dram_out("dbg_coef", [2, 8], F32)
                ctx.dram_out("dbg_gc", [2, 128, 16], F32)
                ctx.dram_out("dbg_g12", [2, 128, 8], BF16)
            b.dma("Q", ctx.t["dbg_coef"][s:s + 1, :], coef[0:1, 0:8],
                  reads=[coef._res], writes=["dbg_coef"], tag="dbgc")
            b.dma("Q", ctx.t["dbg_gc"][s], gc[:, 0:16],
                  reads=[gc._res], writes=["dbg_gc"], tag="dbgg")
            b.dma("Q", ctx.t["dbg_g12"][s], g12[:, 0:8],
                  reads=[g12._res], writes=["dbg_g12"], tag="dbgg2")
        for cyc in range(3):
            for bi in range(NBLK):
                lt0 = bi * BLK
                b.op("V", lambda v, o=obf[:, 0:NC * BLK],
                     i=outr[:].rearrange("p (c t) -> p c t", c=NC)[:, :, lt0:lt0 + BLK]:
                     v.tensor_copy(o, i),
                     reads=[outr._res], writes=[obf._res], tag="ob")
                for m in range(NC):
                    p = pb[m % 4]
                    for k in range(NC):
                        b.op("T", lambda t, pp=p[:, 0:BLK],
                             l=wtile(cw1, k, (m * 128, 128), 1024), r=obv[:, k, :],
                             st=(k == 0), sp=(k == NC - 1):
                             t.matmul(pp, l, r, start=st, stop=sp),
                             reads=[cw1._res, obf._res], writes=[p._res], tag="c1m")
                    b.op("S", lambda ss, o=h1bf[:, m * BLK:(m + 1) * BLK],
                         i=p[:, 0:BLK], bi=vcol(ctx, "cb1", m):
                         ss.activation(o, i, AF.Gelu, bias=bi),
                         reads=[p._res, "vecs"], writes=[h1bf._res], tag="c1g")
                for m in range(4):
                    p = pb[m % 4]
                    for k in range(NC):
                        b.op("T", lambda t, pp=p[:, 0:BLK],
                             l=wtile(cw2, k, (m * 128, 128), 512),
                             r=h1bf[:, k * BLK:(k + 1) * BLK],
                             st=(k == 0), sp=(k == NC - 1):
                             t.matmul(pp, l, r, start=st, stop=sp),
                             reads=[cw2._res, h1bf._res], writes=[p._res], tag="c2m")
                    b.op("S", lambda ss, o=h2bf[:, m * BLK:(m + 1) * BLK],
                         i=p[:, 0:BLK], bi=vcol(ctx, "cb2", m):
                         ss.activation(o, i, AF.Gelu, bias=bi),
                         reads=[p._res, "vecs"], writes=[h2bf._res], tag="c2g")
                for m in range(2):
                    p = pb[m % 4]
                    for k in range(4):
                        b.op("T", lambda t, pp=p[:, 0:BLK],
                             l=wtile(cw3, k, (m * 128, 128), 256),
                             r=h2bf[:, k * BLK:(k + 1) * BLK],
                             st=(k == 0), sp=(k == 3):
                             t.matmul(pp, l, r, start=st, stop=sp),
                             reads=[cw3._res, h2bf._res], writes=[p._res], tag="c3m")
                    b.op("S", lambda ss, o=h3bf[:, m * BLK:(m + 1) * BLK],
                         i=p[:, 0:BLK], bi=vcol(ctx, "cb3", m):
                         ss.activation(o, i, AF.Gelu, bias=bi),
                         reads=[p._res, "vecs"], writes=[h3bf._res], tag="c3g")
                for k in range(2):
                    b.op("T", lambda t, pp=pstat[0:1, 0:BLK],
                         l=wtile(cw4, k, (0, 1), 1),
                         r=h3bf[:, k * BLK:(k + 1) * BLK],
                         st=(k == 0), sp=(k == 1):
                         t.matmul(pp, l, r, start=st, stop=sp),
                         reads=[cw4._res, h3bf._res], writes=[pstat._res], tag="c4m")
                b.op("S", lambda ss, o=cpl[0:1, 0:BLK], i=pstat[0:1, 0:BLK],
                     bi=vcol_row0(ctx, "sc", 0):
                     ss.activation(o, i, AF.Sigmoid, bias=bi),
                     reads=[pstat._res, "vecs"], writes=[cpl._res], tag="tf")
                b.op("V", lambda v, o=cpl[0:1, 0:BLK], i=cpl[0:1, 0:BLK],
                     s1=coef[0:1, 0:1], s2=coef[0:1, 1:2]:
                     v.tensor_scalar(o, i, s1, s2, ALU.mult, ALU.add),
                     reads=[cpl._res, coef._res], writes=[cpl._res], tag="cpl")
                b.op("V", lambda v, o=cpl[0:1, BLK:2 * BLK], i=cpl[0:1, 0:BLK]:
                     v.tensor_scalar_mul(o, i, -1.0),
                     reads=[cpl._res], writes=[cpl._res], tag="ncpl")
                b.op("T", lambda t, p=pbb[0][:, 0:BLK], l=ctx.t["onesf"][0:1, :],
                     r=cpl[0:1, 0:BLK]:
                     t.matmul(p, l, r, start=True, stop=True),
                     reads=[cpl._res, "onesf"], writes=[pbb[0]._res], tag="bcp")
                b.op("T", lambda t, p=pbb[1][:, 0:BLK], l=ctx.t["onesf"][0:1, :],
                     r=cpl[0:1, BLK:2 * BLK]:
                     t.matmul(p, l, r, start=True, stop=True),
                     reads=[cpl._res, "onesf"], writes=[pbb[1]._res], tag="bcn")
                b.op("S", lambda s_, o=cb[:, 0:BLK], i=pbb[0][:, 0:BLK]:
                     s_.activation(o, i, AF.Copy),
                     reads=[pbb[0]._res], writes=[cb._res], tag="cbp")
                b.op("S", lambda s_, o=cb[:, BLK:2 * BLK], i=pbb[1][:, 0:BLK]:
                     s_.activation(o, i, AF.Copy),
                     reads=[pbb[1]._res], writes=[cb._res], tag="cbn")
                for (p_, q_, sgn) in ((0, 4, 1), (1, 5, 1), (2, 6, -1), (3, 7, -1)):
                    first = cb[:, 0:BLK] if sgn > 0 else cb[:, BLK:2 * BLK]
                    second = cb[:, BLK:2 * BLK] if sgn > 0 else cb[:, 0:BLK]
                    op_ = outr[:, p_ * SEQ + lt0:p_ * SEQ + lt0 + BLK]
                    oq_ = outr[:, q_ * SEQ + lt0:q_ * SEQ + lt0 + BLK]
                    b.op("V", lambda v, o=tA[:, 0:BLK], a=first, c=oq_:
                         v.tensor_mul(o, a, c),
                         reads=[cb._res, outr._res], writes=[tA._res], tag="tw1")
                    b.op("V", lambda v, o=tB[:, 0:BLK], a=second, c=op_:
                         v.tensor_mul(o, a, c),
                         reads=[cb._res, outr._res], writes=[tB._res], tag="tw2")
                    b.op("V", lambda v, o=op_, a=op_, c=tA[:, 0:BLK]:
                         v.tensor_add(o, a, c),
                         reads=[outr._res, tA._res], writes=[outr._res], tag="tw3")
                    b.op("V", lambda v, o=oq_, a=oq_, c=tB[:, 0:BLK]:
                         v.tensor_add(o, a, c),
                         reads=[outr._res, tB._res], writes=[outr._res], tag="tw4")
        if DEBUG:
            ctx.debug("ocyc", outr, sc0, SEQ, NC, F32)
        for bi in range(NBLK):
            t0 = sc0 + bi * BLK
            lt0 = bi * BLK
            orv = outr[:].rearrange("p (c t) -> p c t", c=NC)
            b.op("V", lambda v, o=obf[:, 0:NC * BLK], i=orv[:, :, lt0:lt0 + BLK]:
                 v.tensor_copy(o, i),
                 reads=[outr._res], writes=[obf._res], tag="ob2")
            u_stats(ctx, obf, sq, BLK, NC, pstat, pstatB, R)
            u_bcast2(ctx, R, BLK, nb, pbb[0], pbb[1])
            u_load_stream(ctx, x3b, x3, t0, BLK, NC, tag="x3ld2")
            for c in range(NC):
                ods = orv[:, c, lt0:lt0 + BLK]
                b.op("V", lambda v, o=stgv[:, c, :], a=ods, n=nb[:, 0:BLK]:
                     v.tensor_mul(o, a, n),
                     reads=[outr._res, nb._res], writes=[stg._res], tag="p1")
                b.op("V", lambda v, o=stgv[:, c, :], a=stgv[:, c, :], n=nb[:, BLK:2 * BLK]:
                     v.tensor_add(o, a, n),
                     reads=[stg._res, nb._res], writes=[stg._res], tag="p2")
                b.op("V", lambda v, o=stgv[:, c, :], a=stgv[:, c, :], cc=x3v[:, c, :]:
                     v.tensor_add(o, a, cc),
                     reads=[stg._res, x3b._res], writes=[stg._res], tag="p3")
            u_store_stream(ctx, stg, x4, t0, BLK, NC, tag="x4st")






def emit_B1(ctx):
    b = ctx.b
    x2, x3 = ctx.t["x2"], ctx.t["x3"]
    wat = ctx.sbuf("wat_sb", [128, 8 * 3072], BF16)
    u_wload(ctx, wat, ctx.t["wattnT"], 8, 0, 3072, nsplit=4)
    wps = [ctx.sbuf(f"b1_wp{i}", [128, 8 * 128], BF16) for i in range(2)]
    Ksb = ctx.sbuf("b1_K", [128, NC * SEQ], BF16)
    Vsb = ctx.sbuf("b1_V", [128, 16 * 1024], BF16)
    xbf = ctx.sbuf("b1_xbf", [128, NC * BLK], BF16)
    ar1 = ctx.sbuf("b1_ar1", [128, 8192], BF16)
    ar2 = ctx.sbuf("b1_ar2", [128, 4096], BF16)
    scr = ctx.sbuf("b1_scr", [128, 2048], F32)
    attn = ctx.sbuf("b1_attn", [128, NC * BLK], BF16)
    nb = ctx.sbuf("b1_nb", [128, 2 * BLK], F32)
    R = mkrows(ctx, "b1", BLK, names=("mu", "m2", "var", "n1", "n0"))
    ctx.sbuf("lntmp", [128, BLK], F32)
    mx = ctx.sbuf("b1_mx", [128, 1], F32)
    sums = ctx.sbuf("b1_sums", [128, 64], F32)
    sumr = ctx.sbuf("b1_sumr", [16, BLK], F32)
    rb = ctx.sbuf("b1_rb", [128, BLK], F32)
    stg = ctx.sbuf("b1_stg", [128, 2 * BLK], F32)
    x2b = ctx.sbuf("b1_x2b", [128, 2 * BLK], F32)
    pts = ctx.sbuf("b1_pts", [128, 1024], BF16)
    oscr = ctx.dram_internal("b1_oscr", [D, T], F32)
    odb = ctx.sbuf("b1_odb", [128, 2 * BLK], F32)
    pb = [ctx.psum(f"b1_pb{i}", [128, BLK]) for i in range(4)]
    pv = [ctx.psum(f"b1_pv{i}", [128, BLK]) for i in range(2)]
    pt = ctx.psum("b1_pt", [128, 1024], BF16)
    iden = ctx.t["iden"]
    idenf = ctx.t["idenf"]
    mneg = ctx.t["mneg"]
    eh = ctx.t["eh"]
    AR1, AR2 = ar1._res, ar2._res
    uv = lambda k: ar1[:, k * BLK:(k + 1) * BLK]
    sqt = lambda: ar1[:, 4096:8192]
    wat_t = lambda k, m: wat[:, k * 3072 + m * 128:k * 3072 + m * 128 + 128]

    for s in range(NSEQ):
        sc0 = s * SEQ
        for g in range(NBLK):
            t0 = sc0 + g * BLK
            bt0 = g * BLK
            u_load_stream(ctx, xbf, x2, t0, BLK, NC, cast=True)
            _stats_from(ctx, xbf, lambda k: sqt()[:, k * BLK:(k + 1) * BLK],
                        BLK, NC, pb[0], pb[1], R, AR1)
            u_bcast2(ctx, R, BLK, nb, pb[2], pb[3])
            for c in range(NC):
                b.op("V", lambda v, o=ctx.t["lntmp"][:, 0:BLK],
                     a=sb3(xbf, BLK, NC)[:, c, :], n=nb[:, 0:BLK]:
                     v.tensor_mul(o, a, n),
                     reads=[xbf._res, nb._res], writes=["lntmp"], tag="lnm")
                b.op("V", lambda v, o=uv(c), a=ctx.t["lntmp"][:, 0:BLK],
                     n=nb[:, BLK:2 * BLK]:
                     v.tensor_add(o, a, n),
                     reads=["lntmp", nb._res], writes=[AR1], tag="lna")
            for m in range(16):
                p = pb[m % 2]
                for k in range(NC):
                    b.op("T", lambda t, pp=p[:, 0:BLK], l=wat_t(k, m), r=uv(k),
                         st=(k == 0), sp=(k == NC - 1):
                         t.matmul(pp, l, r, start=st, stop=sp),
                         reads=[wat._res, AR1], writes=[p._res], tag="qkv")
                if m < 8:
                    b.op("S", lambda s_, o=ar2[:, m * BLK:(m + 1) * BLK],
                         i=p[:, 0:BLK]: s_.activation(o, i, AF.Copy),
                         reads=[p._res], writes=[AR2], tag="qe")
                else:
                    kk = m - 8
                    b.op("S", lambda s_, o=Ksb[:, kk * SEQ + bt0:kk * SEQ + bt0 + BLK],
                         i=p[:, 0:BLK]: s_.activation(o, i, AF.Copy, scale=0.125),
                         reads=[p._res], writes=[f"K{s}_{g}"], tag="ke")
            for tt in range(4):
                for half in range(2):
                    p = pb[2 + half]
                    for k in range(NC):
                        b.op("T", lambda t, pp=p[:, 0:BLK],
                             l=uv(k)[:, tt * 128:(tt + 1) * 128],
                             r=wat[:, k * 3072 + 2048 + half * BLK:
                                   k * 3072 + 2048 + (half + 1) * BLK],
                             st=(k == 0), sp=(k == NC - 1):
                             t.matmul(pp, l, r, start=st, stop=sp),
                             reads=[wat._res, AR1], writes=[p._res], tag="vmm")
                    b.op("S", lambda s_, o=Vsb[:, (g * 4 + tt) * 1024 + half * BLK:
                                               (g * 4 + tt) * 1024 + (half + 1) * BLK],
                         i=p[:, 0:BLK]: s_.activation(o, i, AF.Copy),
                         reads=[p._res], writes=[f"V{s}_{g}"], tag="ve")
            poff = []
            off = 0
            for qt in range(4):
                poff.append(off)
                off += (g * 4 + qt + 1) * 128
            kread = [f"K{s}_{gg}" for gg in range(g + 1)]
            vread = [f"V{s}_{gg}" for gg in range(g + 1)]
            for h in range(H):
                hb = (h % 2) * 64
                kc = h // 2
                for qt in range(4):
                    L = (g * 4 + qt) * 128 + 128
                    n_s = (L + 511) // 512
                    for j in range(n_s):
                        W = min(512, L - j * 512)
                        p = pb[(h + qt + j) % 2]
                        b.op("T", lambda t, pp=p[:, 0:W],
                             l=ar2[hb:hb + 64, kc * BLK + qt * 128:kc * BLK + qt * 128 + 128],
                             r=Ksb[hb:hb + 64, kc * SEQ + j * 512:kc * SEQ + j * 512 + W]:
                             t.matmul(pp, l, r, start=True, stop=True),
                             reads=[AR2] + kread, writes=[p._res], tag="sc")
                        last = (j == n_s - 1)
                        if not last:
                            b.op("S", lambda s_, o=scr[:, j * 512:j * 512 + W],
                                 i=p[:, 0:W]: s_.activation(o, i, AF.Copy),
                                 reads=[p._res], writes=[scr._res], tag="sce")
                        else:
                            if W > 128:
                                b.op("S", lambda s_, o=scr[:, j * 512:j * 512 + W - 128],
                                     i=p[:, 0:W - 128]: s_.activation(o, i, AF.Copy),
                                     reads=[p._res], writes=[scr._res], tag="sce")
                            b.op("V", lambda v, o=scr[:, L - 128:L], a=p[:, W - 128:W],
                                 m_=mneg[:]: v.tensor_add(o, a, m_),
                                 reads=[p._res, "mneg"], writes=[scr._res], tag="scm")
                    b.op("V", lambda v, o=mx[:, 0:1], i=scr[:, 0:L]:
                         v.reduce_max(o, i, axis=AX.X, negate=True),
                         reads=[scr._res], writes=[mx._res], tag="mx")
                    b.op("S", lambda s_, o=ar1[:, poff[qt]:poff[qt] + L],
                         i=scr[:, 0:L], bi=mx[:, 0:1],
                         ac=sums[:, qt * 16 + h:qt * 16 + h + 1]:
                         s_.activation(o, i, AF.Exp, bias=bi, accum_out=ac),
                         reads=[scr._res, mx._res], writes=[AR1, sums._res],
                         tag="exp")
                ppv = pv[(h // 2) % 2]
                row0 = (h % 2) * 64
                tp = (0, 64) if h % 2 else None
                nsc = (g * 4 + 4)

                def emit_tr(sc_):
                    qt0 = max(0, sc_ - 4 * g)
                    half = (sc_ % 2) * 512
                    for qt in range(qt0, 4):
                        b.op("T", lambda t, o=pt[:, half + qt * 128:half + (qt + 1) * 128],
                             i=ar1[:, poff[qt] + sc_ * 128:poff[qt] + (sc_ + 1) * 128],
                             idn=iden[:]: t.transpose(o, i, idn),
                             reads=[AR1, "iden"], writes=[pt._res], tag="tr")
                    b.op("V", lambda v, o=pts[:, half + qt0 * 128:half + 512],
                         i=pt[:, half + qt0 * 128:half + 512]: v.tensor_copy(o, i),
                         reads=[pt._res], writes=[pts._res], tag="pte")

                emit_tr(0)
                for sc_ in range(nsc):
                    if sc_ + 1 < nsc:
                        emit_tr(sc_ + 1)
                    qt0 = max(0, sc_ - 4 * g)
                    half = (sc_ % 2) * 512
                    b.op("T", lambda t, pp=ppv[row0:row0 + 64, qt0 * 128:512],
                         l=Vsb[:, sc_ * 1024 + h * 64:sc_ * 1024 + h * 64 + 64],
                         r=pts[:, half + qt0 * 128:half + 512],
                         st=(sc_ == 0), sp=(sc_ == nsc - 1), tpos=tp:
                         t.matmul(pp, l, r, start=st, stop=sp, tile_position=tpos),
                         reads=[pts._res] + vread, writes=[ppv._res], tag="pv")
                if h % 2 == 1:
                    c = h // 2
                    b.op("S", lambda s_, o=attn[:, c * BLK:(c + 1) * BLK],
                         i=ppv[:, 0:BLK]: s_.activation(o, i, AF.Copy),
                         reads=[ppv._res], writes=[attn._res], tag="pve")
            for qt in range(4):
                b.op("T", lambda t, o=pb[2][0:16, 0:128],
                     i=sums[:, qt * 16:(qt + 1) * 16], idn=idenf[:]:
                     t.transpose(o, i, idn),
                     reads=[sums._res, "idenf"], writes=[pb[2]._res], tag="str")
                b.op("V", lambda v, o=sumr[0:16, qt * 128:(qt + 1) * 128],
                     i=pb[2][0:16, 0:128]: v.tensor_copy(o, i),
                     reads=[pb[2]._res], writes=[sumr._res], tag="ste")
            b.op("V", lambda v, o=sumr[0:16, 0:BLK], i=sumr[0:16, 0:BLK]:
                 v.reciprocal(o, i), reads=[sumr._res], writes=[sumr._res], tag="rec")
            for c in range(NC):
                p = pb[3 if c % 2 else 2]
                b.op("T", lambda t, pp=p[:, 0:BLK], l=eh[0:16, c * 128:(c + 1) * 128],
                     r=sumr[0:16, 0:BLK]: t.matmul(pp, l, r, start=True, stop=True),
                     reads=[sumr._res, "eh"], writes=[p._res], tag="rbm")
                b.op("S", lambda s_, o=rb[:, 0:BLK], i=p[:, 0:BLK]:
                     s_.activation(o, i, AF.Copy),
                     reads=[p._res], writes=[rb._res], tag="rbe")
                b.op("V", lambda v, o=attn[:, c * BLK:(c + 1) * BLK],
                     a=attn[:, c * BLK:(c + 1) * BLK], n=rb[:, 0:BLK]:
                     v.tensor_mul(o, a, n),
                     reads=[attn._res, rb._res], writes=[attn._res], tag="rnm")
            for m in range(NC):
                wp = wps[m % 2]
                wsrc = ctx.t["wprojT1"].ap().rearrange("(kc p) o -> p kc o", p=128)
                b.dma("G", wp[:, 0:8 * 128].rearrange("p (k o) -> p k o", k=8),
                      wsrc[:, :, m * 128:(m + 1) * 128],
                      reads=["wprojT1"], writes=[wp._res], tag="wpld")
                p = pv[m % 2]
                for k in range(NC):
                    b.op("T", lambda t, pp=p[:, 0:BLK],
                         l=wp[:, k * 128:(k + 1) * 128],
                         r=attn[:, k * BLK:(k + 1) * BLK],
                         st=(k == 0), sp=(k == NC - 1):
                         t.matmul(pp, l, r, start=st, stop=sp),
                         reads=[wp._res, attn._res], writes=[p._res], tag="pj")
                oh = (m % 2) * BLK
                b.op("S", lambda s_, o=odb[:, oh:oh + BLK], i=p[:, 0:BLK]:
                     s_.activation(o, i, AF.Copy),
                     reads=[p._res], writes=[odb._res], tag="oe")
                b.op("G", lambda g_, o=xbf[:, m * BLK:(m + 1) * BLK],
                     i=odb[:, oh:oh + BLK]: g_.tensor_copy(o, i),
                     reads=[odb._res], writes=[xbf._res], tag="obf")
                b.dma("Q", stream_view(oscr, t0, BLK)[:, m, :],
                      odb[:, oh:oh + BLK],
                      reads=[odb._res], writes=[oscr._res], tag="ost")
            _stats_from(ctx, xbf, lambda k: sqt()[:, k * BLK:(k + 1) * BLK],
                        BLK, NC, pb[0], pb[1], R, AR1)
            u_bcast2(ctx, R, BLK, nb, pb[2], pb[3])
            for c in range(NC):
                half = (c % 2) * BLK
                b.dma("Q", odb[:, half:half + BLK],
                      stream_view(oscr, t0, BLK)[:, c, :],
                      reads=[oscr._res], writes=[odb._res], tag="old")
                b.dma("Q", x2b[:, half:half + BLK],
                      stream_view(x2, t0, BLK)[:, c, :],
                      reads=[x2._res], writes=[x2b._res], tag="x2c")
                b.op("V", lambda v, o=stg[:, half:half + BLK],
                     a=odb[:, half:half + BLK], n=nb[:, 0:BLK]:
                     v.tensor_mul(o, a, n),
                     reads=[odb._res, nb._res], writes=[stg._res], tag="p1")
                b.op("V", lambda v, o=stg[:, half:half + BLK],
                     a=stg[:, half:half + BLK], n=nb[:, BLK:2 * BLK]:
                     v.tensor_add(o, a, n),
                     reads=[stg._res, nb._res], writes=[stg._res], tag="p2")
                b.op("V", lambda v, o=stg[:, half:half + BLK],
                     a=stg[:, half:half + BLK],
                     cc=x2b[:, half:half + BLK]:
                     v.tensor_add(o, a, cc),
                     reads=[stg._res, x2b._res], writes=[stg._res], tag="p3")
                b.dma("Q", stream_view(x3, t0, BLK)[:, c, :],
                      stg[:, half:half + BLK],
                      reads=[stg._res], writes=[x3._res], tag="x3st")


def _stats_from(ctx, xbf, sqf, w, nch, pstatA, pstatB, R, sq_res,
                square_src=None, inner=False):
    b = ctx.b
    statl = ctx.t["statl"]
    epsc = ctx.t["epsc"]
    src = square_src if square_src is not None else xbf
    for k in range(nch):
        b.op("S", lambda s, o=sqf(k), i=src[:, k * w:(k + 1) * w]:
             s.activation(o, i, AF.Square),
             reads=[src._res], writes=[sq_res], tag="sq")
    for k in range(nch):
        b.op("T", lambda t, p=pstatA[0:1, 0:w], l=statl[:],
             r=xbf[:, k * w:(k + 1) * w],
             st=(k == 0), sp=(k == nch - 1):
             t.matmul(p, l, r, start=st, stop=sp),
             reads=[xbf._res, "statl"], writes=[pstatA._res], tag="mu")
    for k in range(nch):
        b.op("T", lambda t, p=pstatB[0:1, 0:w], l=statl[:], r=sqf(k),
             st=(k == 0), sp=(k == nch - 1):
             t.matmul(p, l, r, start=st, stop=sp),
             reads=[sq_res, "statl"], writes=[pstatB._res], tag="m2")
    mu, m2, var, n1, n0 = R["mu"], R["m2"], R["var"], R["n1"], R["n0"]
    b.op("V", lambda v, o=mu[0:1, 0:w], i=pstatA[0:1, 0:w]: v.tensor_copy(o, i),
         reads=[pstatA._res], writes=[mu._res], tag="evmu")
    b.op("V", lambda v, o=m2[0:1, 0:w], i=pstatB[0:1, 0:w]: v.tensor_copy(o, i),
         reads=[pstatB._res], writes=[m2._res], tag="evm2")
    b.op("V", lambda v, o=var[0:1, 0:w], a=mu[0:1, 0:w]: v.tensor_mul(o, a, a),
         reads=[mu._res], writes=[var._res], tag="mu2")
    b.op("V", lambda v, o=var[0:1, 0:w], a=m2[0:1, 0:w], c=var[0:1, 0:w]:
         v.tensor_sub(o, a, c), reads=[m2._res, var._res], writes=[var._res])
    b.op("S", lambda s, o=n1[0:1, 0:w], i=var[0:1, 0:w]:
         s.activation(o, i, AF.Ln, bias=epsc[0:1, :]),
         reads=[var._res, "epsc"], writes=[n1._res], tag="ln")
    b.op("S", lambda s, o=n1[0:1, 0:w], i=n1[0:1, 0:w]:
         s.activation(o, i, AF.Exp, scale=-0.5),
         reads=[n1._res], writes=[n1._res], tag="exp")
    if inner:
        i2, istd, c0 = R["i2"], R["istd"], R["c0"]
        b.op("V", lambda v, o=i2[0:1, 0:w], i=var[0:1, 0:w]:
             v.tensor_scalar_add(o, i, EPS), reads=[var._res], writes=[i2._res])
        b.op("V", lambda v, o=i2[0:1, 0:w], i=i2[0:1, 0:w]:
             v.reciprocal(o, i), reads=[i2._res], writes=[i2._res])
        b.op("V", lambda v, o=i2[0:1, 0:w], a=var[0:1, 0:w], c=i2[0:1, 0:w]:
             v.tensor_mul(o, a, c), reads=[var._res, i2._res], writes=[i2._res])
        b.op("S", lambda s, o=i2[0:1, 0:w], i=i2[0:1, 0:w]:
             s.activation(o, i, AF.Ln, bias=epsc[0:1, :]),
             reads=[i2._res, "epsc"], writes=[i2._res])
        b.op("S", lambda s, o=i2[0:1, 0:w], i=i2[0:1, 0:w]:
             s.activation(o, i, AF.Exp, scale=-0.5),
             reads=[i2._res], writes=[i2._res])
        b.op("S", lambda s, o=istd[0:1, 0:w], i=n1[0:1, 0:w]:
             s.activation(o, i, AF.Copy), reads=[n1._res], writes=[istd._res])
        b.op("V", lambda v, o=c0[0:1, 0:w], i=mu[0:1, 0:w], c=istd[0:1, 0:w]:
             v.scalar_tensor_tensor(o, i, -1.0, c, ALU.mult, ALU.mult),
             reads=[mu._res, istd._res], writes=[c0._res], tag="c0")
        b.op("V", lambda v, o=n1[0:1, 0:w], a=n1[0:1, 0:w], c=i2[0:1, 0:w]:
             v.tensor_mul(o, a, c), reads=[i2._res], writes=[n1._res])
    b.op("V", lambda v, o=n0[0:1, 0:w], i=mu[0:1, 0:w], c=n1[0:1, 0:w]:
         v.scalar_tensor_tensor(o, i, -1.0, c, ALU.mult, ALU.mult),
         reads=[mu._res, n1._res], writes=[n0._res], tag="n0")






def emit_A1(ctx):
    b = ctx.b
    x0, xa1 = ctx.t["x0"], ctx.t["xa1"]
    wqh = ctx.t["wqkvfT"]
    wk = ctx.sbuf("a1_wk", [128, 8 * 1024], BF16)
    wv = ctx.sbuf("a1_wv", [128, 8 * 1024], BF16)
    u_wload(ctx, wk, wqh, 8, 1024, 2048, nsplit=2, tag="wkld")
    u_wload(ctx, wv, wqh, 8, 2048, 3072, nsplit=2, tag="wvld")
    wst = [ctx.sbuf(f"a1_ws{i}", [128, 8 * 128], BF16) for i in range(2)]
    Ksb = ctx.sbuf("a1_K", [128, NC * SEQ], BF16)
    Vsb = ctx.sbuf("a1_V", [128, 16 * 1024], BF16)
    xbf = ctx.sbuf("a1_xbf", [128, NC * BLK], BF16)
    ar1 = ctx.sbuf("a1_ar1", [128, 8192], BF16)
    ar2 = ctx.sbuf("a1_ar2", [128, 4096], BF16)
    scr = ctx.sbuf("a1_scr", [128, 2052], F32)
    attn = xbf
    nb = ctx.sbuf("a1_nb", [128, 2 * BLK], F32)
    R = mkrows(ctx, "a1", BLK)
    ctx.sbuf("lntmp", [128, BLK], F32)
    mx = ctx.sbuf("a1_mx", [128, 1], F32)
    sums = ctx.sbuf("a1_sums", [128, 64], F32)
    sumr = ctx.sbuf("a1_sumr", [16, BLK], F32)
    rb = ctx.sbuf("a1_rb", [128, BLK], F32)
    pfb = rb
    stg = ctx.sbuf("a1_stg", [128, 2 * BLK], F32)
    x0f = ctx.sbuf("a1_x0f", [128, 2 * BLK], F32)
    pts = ctx.sbuf("a1_pts", [128, 1024], BF16)
    Fsb = ctx.sbuf("a1_F", [128, NC * BLK], BF16)
    fa = ctx.sbuf("a1_fa", [64, BLK], BF16)
    fat = ctx.sbuf("a1_fat", [128, 4 * 64], BF16)
    pfut = ctx.sbuf("a1_pfut", [128, 4 * 64], BF16)
    pfs = ctx.sbuf("a1_pfs", [64, BLK], BF16)
    prod = ctx.sbuf("a1_prod", [128, BLK], BF16)
    pb = [ctx.psum(f"a1_pb{i}", [128, BLK]) for i in range(4)]
    pv = [ctx.psum(f"a1_pv{i}", [128, BLK]) for i in range(2)]
    pt = ctx.psum("a1_pt", [128, 1024], BF16)
    iden, idenf, mneg = ctx.t["iden"], ctx.t["idenf"], ctx.t["mneg"]
    eh, e2, efa, wcv = ctx.t["eh"], ctx.t["e2"], ctx.t["efa"], ctx.t["wconv_c"]
    AR1, AR2 = ar1._res, ar2._res
    uv = lambda k: ar1[:, k * BLK:(k + 1) * BLK]
    sqf = lambda k: ar1[:, 4096 + k * BLK:4096 + (k + 1) * BLK]
    ufv = lambda k: ar1[:, k * BLK:(k + 1) * BLK]
    kfv = lambda m: ar1[:, 4096 + m * BLK:4096 + (m + 1) * BLK]
    fut = lambda c: ar1[:, 4096 + c * BLK:4096 + (c + 1) * BLK]
    wqt = lambda buf, k, m: buf[:, k * 128:(k + 1) * 128]
    wkt = lambda k, m: wk[:, k * 1024 + m * 128:k * 1024 + (m + 1) * 128]
    wvt = lambda k, m: wv[:, k * 1024 + m * 128:k * 1024 + (m + 1) * 128]
    wsrc = wqh.ap().rearrange("(kc p) o -> p kc o", p=128)
    wpsrc = ctx.t["wprojT0"].ap().rearrange("(kc p) o -> p kc o", p=128)

    nseq, nblk = LIMIT_BLOCKS or (NSEQ, NBLK)
    for s in range(nseq):
        sc0 = s * SEQ
        for g in range(nblk):
            t0 = sc0 + g * BLK
            bt0 = g * BLK
            u_load_stream(ctx, xbf, x0, t0, BLK, NC, cast=True)
            _stats_from(ctx, xbf, sqf, BLK, NC, pb[0], pb[1], R, AR1, inner=True)
            u_bcast2(ctx, R, BLK, nb, pb[2], pb[3])
            for c in range(NC):
                b.op("V", lambda v, o=ctx.t["lntmp"][:, 0:BLK],
                     a=sb3(xbf, BLK, NC)[:, c, :], n=nb[:, 0:BLK]:
                     v.tensor_mul(o, a, n),
                     reads=[xbf._res, nb._res], writes=["lntmp"], tag="lnm")
                b.op("V", lambda v, o=ctx.t["lntmp"][:, 0:BLK],
                     a=ctx.t["lntmp"][:, 0:BLK], n=nb[:, BLK:2 * BLK]:
                     v.tensor_add(o, a, n),
                     reads=["lntmp", nb._res], writes=["lntmp"], tag="lna")
                b.op("V", lambda v, o=uv(c), i=ctx.t["lntmp"][:, 0:BLK],
                     gg=vcol(ctx, "g1", c), bb=vcol(ctx, "b1", c):
                     v.tensor_scalar(o, i, gg, bb, ALU.mult, ALU.add),
                     reads=["lntmp", "vecs"], writes=[AR1], tag="lngb")
            if DEBUG:
                ctx.debug("u", TH(ar1.h, AR1), t0, BLK, NC, BF16)
            for m in range(32):
                if 8 <= m < 16:
                    tile = lambda k, mm=m: wkt(k, mm - 8)
                    wres = wk._res
                elif 16 <= m < 24:
                    continue
                else:
                    wbuf = wst[(m if m < 8 else m - 16) % 2]
                    wv_ = wbuf[:, 0:8 * 128].rearrange("p (k o) -> p k o", k=8)
                    b.dma("G", wv_, wsrc[:, :, m * 128:(m + 1) * 128],
                          reads=[wqh._res], writes=[wbuf._res], tag="wsld")
                    tile = lambda k, bu=wbuf: wqt(bu, k, 0)
                    wres = wbuf._res
                p = pb[m % 2]
                for k in range(NC):
                    b.op("T", lambda t, pp=p[:, 0:BLK], l=tile(k), r=uv(k),
                         st=(k == 0), sp=(k == NC - 1):
                         t.matmul(pp, l, r, start=st, stop=sp),
                         reads=[wres, AR1], writes=[p._res], tag="qkvf")
                if m < 8:
                    b.op("S", lambda s_, o=ar2[:, m * BLK:(m + 1) * BLK],
                         i=p[:, 0:BLK]: s_.activation(o, i, AF.Copy),
                         reads=[p._res], writes=[AR2], tag="qe")
                elif m < 16:
                    kk = m - 8
                    b.op("S", lambda s_, o=Ksb[:, kk * SEQ + bt0:kk * SEQ + bt0 + BLK],
                         i=p[:, 0:BLK]: s_.activation(o, i, AF.Copy, scale=0.125),
                         reads=[p._res], writes=[f"K{s}_{g}"], tag="ke")
                else:
                    ff = m - 24
                    b.op("S", lambda s_, o=Fsb[:, ff * BLK:(ff + 1) * BLK],
                         i=p[:, 0:BLK]: s_.activation(o, i, AF.Copy),
                         reads=[p._res], writes=[Fsb._res], tag="fe")
            for tt in range(4):
                for half in range(2):
                    p = pb[2 + half]
                    for k in range(NC):
                        b.op("T", lambda t, pp=p[:, 0:BLK],
                             l=uv(k)[:, tt * 128:(tt + 1) * 128],
                             r=wv[:, k * 1024 + half * BLK:k * 1024 + (half + 1) * BLK],
                             st=(k == 0), sp=(k == NC - 1):
                             t.matmul(pp, l, r, start=st, stop=sp),
                             reads=[wv._res, AR1], writes=[p._res], tag="vmm")
                    b.op("S", lambda s_, o=Vsb[:, (g * 4 + tt) * 1024 + half * BLK:
                                               (g * 4 + tt) * 1024 + (half + 1) * BLK],
                         i=p[:, 0:BLK]: s_.activation(o, i, AF.Copy),
                         reads=[p._res], writes=[f"V{s}_{g}"], tag="ve")
            if DEBUG:
                ctx.debug("q", TH(ar2.h, AR2), t0, BLK, NC, BF16)
                ctx.debug("f", Fsb, t0, BLK, NC, BF16)
            for fi in range(FUT):
                for c in range(NC):
                    p = pb[c % 2]
                    b.op("T", lambda t, pp=p[0:64, 0:BLK],
                         l=wcv[0:64, fi * 64:(fi + 1) * 64],
                         r=Fsb[0:64, c * BLK:(c + 1) * BLK]:
                         t.matmul(pp, l, r, start=True, stop=True),
                         reads=["wconv_c", Fsb._res], writes=[p._res], tag="uf0")
                    b.op("T", lambda t, pp=p[64:128, 0:BLK],
                         l=wcv[64:128, fi * 64:(fi + 1) * 64],
                         r=Fsb[64:128, c * BLK:(c + 1) * BLK],
                         tpos=(64, 64):
                         t.matmul(pp, l, r, start=True, stop=True,
                                  tile_position=tpos),
                         reads=["wconv_c", Fsb._res], writes=[p._res], tag="uf1")
                    b.op("S", lambda s_, o=ufv(c), i=p[:, 0:BLK]:
                         s_.activation(o, i, AF.Copy),
                         reads=[p._res], writes=[AR1], tag="ufe")
                for m in range(NC):
                    p = pb[2 + m % 2]
                    for k in range(NC):
                        b.op("T", lambda t, pp=p[:, 0:BLK], l=wkt(k, m), r=ufv(k),
                             st=(k == 0), sp=(k == NC - 1):
                             t.matmul(pp, l, r, start=st, stop=sp),
                             reads=[wk._res, AR1], writes=[p._res], tag="kfm")
                    b.op("S", lambda s_, o=kfv(m), i=p[:, 0:BLK]:
                         s_.activation(o, i, AF.Copy),
                         reads=[p._res], writes=[AR1], tag="kfe")
                for c in range(NC):
                    b.op("V", lambda v, o=prod[:, 0:BLK],
                         a=ar2[:, c * BLK:(c + 1) * BLK], cc=kfv(c):
                         v.tensor_mul(o, a, cc),
                         reads=[AR2, AR1], writes=[prod._res], tag="prod")
                    b.op("T", lambda t, pp=pv[0][0:64, 0:BLK],
                         l=efa[:, (4 * c + fi) * 64:(4 * c + fi + 1) * 64],
                         r=prod[:, 0:BLK],
                         st=(fi == 0 and c == 0), sp=(fi == FUT - 1 and c == NC - 1):
                         t.matmul(pp, l, r, start=st, stop=sp),
                         reads=["efa", prod._res], writes=[pv[0]._res], tag="fam")
            b.op("S", lambda s_, o=fa[0:64, 0:BLK], i=pv[0][0:64, 0:BLK]:
                 s_.activation(o, i, AF.Copy),
                 reads=[pv[0]._res], writes=[fa._res], tag="fae")
            for qt in range(4):
                b.op("T", lambda t, o=pt[:, qt * 64:(qt + 1) * 64],
                     i=fa[0:64, qt * 128:(qt + 1) * 128], idn=iden[0:64, 0:64]:
                     t.transpose(o, i, idn),
                     reads=[fa._res, "iden"], writes=[pt._res], tag="fatr")
            b.op("V", lambda v, o=fat[:, 0:256], i=pt[:, 0:256]: v.tensor_copy(o, i),
                 reads=[pt._res], writes=[fat._res], tag="fate")
            if DEBUG:
                if "dbg_fa" not in ctx.t:
                    ctx.dram_out("dbg_fa", [64, T], BF16)
                b.dma("Q", ctx.t["dbg_fa"][0:64, t0:t0 + BLK], fa[0:64, 0:BLK],
                      reads=[fa._res], writes=["dbg_fa"], tag="dbgfa")
            poff = []
            off = 0
            for qt in range(4):
                poff.append(off)
                off += (g * 4 + qt + 1) * 128 + 4
            kread = [f"K{s}_{gg}" for gg in range(g + 1)]
            vread = [f"V{s}_{gg}" for gg in range(g + 1)]
            for h in range(H):
                hb = (h % 2) * 64
                kc = h // 2
                for qt in range(4):
                    L = (g * 4 + qt) * 128 + 128
                    n_s = (L + 511) // 512
                    for j in range(n_s):
                        W = min(512, L - j * 512)
                        p = pb[(h + qt + j) % 2]
                        b.op("T", lambda t, pp=p[:, 0:W],
                             l=ar2[hb:hb + 64, kc * BLK + qt * 128:kc * BLK + qt * 128 + 128],
                             r=Ksb[hb:hb + 64, kc * SEQ + j * 512:kc * SEQ + j * 512 + W]:
                             t.matmul(pp, l, r, start=True, stop=True),
                             reads=[AR2] + kread, writes=[p._res], tag="sc")
                        last = (j == n_s - 1)
                        if not last:
                            b.op("S", lambda s_, o=scr[:, j * 512:j * 512 + W],
                                 i=p[:, 0:W]: s_.activation(o, i, AF.Copy),
                                 reads=[p._res], writes=[scr._res], tag="sce")
                        else:
                            if W > 128:
                                b.op("S", lambda s_, o=scr[:, j * 512:j * 512 + W - 128],
                                     i=p[:, 0:W - 128]: s_.activation(o, i, AF.Copy),
                                     reads=[p._res], writes=[scr._res], tag="sce")
                            b.op("V", lambda v, o=scr[:, L - 128:L], a=p[:, W - 128:W],
                                 m_=mneg[:]: v.tensor_add(o, a, m_),
                                 reads=[p._res, "mneg"], writes=[scr._res], tag="scm")
                    b.op("V", lambda v, o=scr[:, L:L + 4],
                         i=fat[:, qt * 64 + 4 * h:qt * 64 + 4 * h + 4]:
                         v.tensor_copy(o, i),
                         reads=[fat._res], writes=[scr._res], tag="scf")
                    b.op("V", lambda v, o=mx[:, 0:1], i=scr[:, 0:L + 4]:
                         v.reduce_max(o, i, axis=AX.X, negate=True),
                         reads=[scr._res], writes=[mx._res], tag="mx")
                    b.op("S", lambda s_, o=ar1[:, poff[qt]:poff[qt] + L + 4],
                         i=scr[:, 0:L + 4], bi=mx[:, 0:1],
                         ac=sums[:, qt * 16 + h:qt * 16 + h + 1]:
                         s_.activation(o, i, AF.Exp, bias=bi, accum_out=ac),
                         reads=[scr._res, mx._res], writes=[AR1, sums._res],
                         tag="exp")
                    b.op("V", lambda v, o=pfut[:, qt * 64 + 4 * h:qt * 64 + 4 * h + 4],
                         i=ar1[:, poff[qt] + L:poff[qt] + L + 4]:
                         v.tensor_copy(o, i),
                         reads=[AR1], writes=[pfut._res], tag="pfc")
                ppv = pv[(h // 2) % 2]
                row0 = (h % 2) * 64
                tp = (0, 64) if h % 2 else None
                nsc = (g * 4 + 4)

                def emit_tr(sc_):
                    qt0 = max(0, sc_ - 4 * g)
                    half = (sc_ % 2) * 512
                    for qt in range(qt0, 4):
                        b.op("T", lambda t, o=pt[:, half + qt * 128:half + (qt + 1) * 128],
                             i=ar1[:, poff[qt] + sc_ * 128:poff[qt] + (sc_ + 1) * 128],
                             idn=iden[:]: t.transpose(o, i, idn),
                             reads=[AR1, "iden"], writes=[pt._res], tag="tr")
                    b.op("V", lambda v, o=pts[:, half + qt0 * 128:half + 512],
                         i=pt[:, half + qt0 * 128:half + 512]: v.tensor_copy(o, i),
                         reads=[pt._res], writes=[pts._res], tag="pte")

                emit_tr(0)
                for sc_ in range(nsc):
                    if sc_ + 1 < nsc:
                        emit_tr(sc_ + 1)
                    qt0 = max(0, sc_ - 4 * g)
                    half = (sc_ % 2) * 512
                    b.op("T", lambda t, pp=ppv[row0:row0 + 64, qt0 * 128:512],
                         l=Vsb[:, sc_ * 1024 + h * 64:sc_ * 1024 + h * 64 + 64],
                         r=pts[:, half + qt0 * 128:half + 512],
                         st=(sc_ == 0), sp=(sc_ == nsc - 1), tpos=tp:
                         t.matmul(pp, l, r, start=st, stop=sp, tile_position=tpos),
                         reads=[pts._res] + vread, writes=[ppv._res], tag="pv")
                if h % 2 == 1:
                    c = h // 2
                    b.op("S", lambda s_, o=attn[:, c * BLK:(c + 1) * BLK],
                         i=ppv[:, 0:BLK]: s_.activation(o, i, AF.Copy),
                         reads=[ppv._res], writes=[attn._res], tag="pve")
            for qt in range(4):
                b.op("T", lambda t, o=pb[2][0:16, 0:128],
                     i=sums[:, qt * 16:(qt + 1) * 16], idn=idenf[:]:
                     t.transpose(o, i, idn),
                     reads=[sums._res, "idenf"], writes=[pb[2]._res], tag="str")
                b.op("V", lambda v, o=sumr[0:16, qt * 128:(qt + 1) * 128],
                     i=pb[2][0:16, 0:128]: v.tensor_copy(o, i),
                     reads=[pb[2]._res], writes=[sumr._res], tag="ste")
            b.op("V", lambda v, o=sumr[0:16, 0:BLK], i=sumr[0:16, 0:BLK]:
                 v.reciprocal(o, i), reads=[sumr._res], writes=[sumr._res], tag="rec")
            for qt in range(4):
                b.op("T", lambda t, o=pt[0:64, qt * 128:(qt + 1) * 128],
                     i=pfut[:, qt * 64:(qt + 1) * 64], idn=iden[:]:
                     t.transpose(o, i, idn),
                     reads=[pfut._res, "iden"], writes=[pt._res], tag="pftr")
            b.op("V", lambda v, o=pfs[0:64, 0:BLK], i=pt[0:64, 0:BLK]:
                 v.tensor_copy(o, i),
                 reads=[pt._res], writes=[pfs._res], tag="pfse")
            for halfc in range(2):
                for fi in range(FUT):
                    for c in range(NC):
                        p = pb[c % 2]
                        b.op("T", lambda t, pp=p[0:64, 0:BLK],
                             l=wcv[0:64, fi * 64:(fi + 1) * 64],
                             r=Fsb[0:64, c * BLK:(c + 1) * BLK]:
                             t.matmul(pp, l, r, start=True, stop=True),
                             reads=["wconv_c", Fsb._res], writes=[p._res], tag="uf0b")
                        b.op("T", lambda t, pp=p[64:128, 0:BLK],
                             l=wcv[64:128, fi * 64:(fi + 1) * 64],
                             r=Fsb[64:128, c * BLK:(c + 1) * BLK], tpos=(64, 64):
                             t.matmul(pp, l, r, start=True, stop=True,
                                      tile_position=tpos),
                             reads=["wconv_c", Fsb._res], writes=[p._res], tag="uf1b")
                        b.op("S", lambda s_, o=ufv(c), i=p[:, 0:BLK]:
                             s_.activation(o, i, AF.Copy),
                             reads=[p._res], writes=[AR1], tag="ufeb")
                    for cc in range(4):
                        c = halfc * 4 + cc
                        p = pb[2 + cc % 2]
                        for k in range(NC):
                            b.op("T", lambda t, pp=p[:, 0:BLK], l=wvt(k, c), r=ufv(k),
                                 st=(k == 0), sp=(k == NC - 1):
                                 t.matmul(pp, l, r, start=st, stop=sp),
                                 reads=[wv._res, AR1], writes=[p._res], tag="vfm")
                        pp2 = pb[(cc + 1) % 2]
                        b.op("T", lambda t, pp=pp2[:, 0:BLK],
                             l=e2[:, (4 * c + fi) * 128:(4 * c + fi + 1) * 128],
                             r=pfs[0:64, 0:BLK]:
                             t.matmul(pp, l, r, start=True, stop=True),
                             reads=["e2", pfs._res], writes=[pp2._res], tag="pfbm")
                        b.op("S", lambda s_, o=pfb[:, 0:BLK], i=pp2[:, 0:BLK]:
                             s_.activation(o, i, AF.Copy),
                             reads=[pp2._res], writes=[pfb._res], tag="pfbe")
                        fslot = scr[:, cc * BLK:(cc + 1) * BLK]
                        if fi == 0:
                            b.op("V", lambda v, o=fslot, a=p[:, 0:BLK],
                                 cc_=pfb[:, 0:BLK]: v.tensor_mul(o, a, cc_),
                                 reads=[p._res, pfb._res], writes=[scr._res],
                                 tag="fac0")
                        else:
                            b.op("V", lambda v, o=ctx.t["lntmp"][:, 0:BLK],
                                 a=p[:, 0:BLK], cc_=pfb[:, 0:BLK]:
                                 v.tensor_mul(o, a, cc_),
                                 reads=[p._res, pfb._res], writes=["lntmp"],
                                 tag="vfw")
                            b.op("V", lambda v, o=fslot, a=fslot,
                                 cc_=ctx.t["lntmp"][:, 0:BLK]:
                                 v.tensor_add(o, a, cc_),
                                 reads=["lntmp", scr._res], writes=[scr._res],
                                 tag="faca")
                for cc in range(4):
                    c = halfc * 4 + cc
                    p = pb[3 if cc % 2 else 2]
                    b.op("T", lambda t, pp=p[:, 0:BLK],
                         l=eh[0:16, c * 128:(c + 1) * 128],
                         r=sumr[0:16, 0:BLK]: t.matmul(pp, l, r, start=True,
                                                       stop=True),
                         reads=[sumr._res, "eh"], writes=[p._res], tag="rbm")
                    b.op("S", lambda s_, o=rb[:, 0:BLK], i=p[:, 0:BLK]:
                         s_.activation(o, i, AF.Copy),
                         reads=[p._res], writes=[rb._res], tag="rbe")
                    b.op("V", lambda v, o=ctx.t["lntmp"][:, 0:BLK],
                         a=attn[:, c * BLK:(c + 1) * BLK],
                         cc_=scr[:, cc * BLK:(cc + 1) * BLK]:
                         v.tensor_add(o, a, cc_),
                         reads=[attn._res, scr._res], writes=["lntmp"], tag="cmb1")
                    b.op("V", lambda v, o=attn[:, c * BLK:(c + 1) * BLK],
                         a=ctx.t["lntmp"][:, 0:BLK], n=rb[:, 0:BLK]:
                         v.tensor_mul(o, a, n),
                         reads=["lntmp", rb._res], writes=[attn._res], tag="cmb2")
            u_bcast_row(ctx, R["istd"], BLK, pb[0])
            u_bcast_row(ctx, R["c0"], BLK, pb[1])
            b.op("S", lambda s_, o=nb[:, 0:BLK], i=pb[0][:, 0:BLK]:
                 s_.activation(o, i, AF.Copy),
                 reads=[pb[0]._res], writes=[nb._res], tag="i0e")
            b.op("S", lambda s_, o=nb[:, BLK:2 * BLK], i=pb[1][:, 0:BLK]:
                 s_.activation(o, i, AF.Copy),
                 reads=[pb[1]._res], writes=[nb._res], tag="c0e")
            if DEBUG:
                ctx.debug("nb", nb, t0, BLK, 2, F32)
                if "dbg_rows" not in ctx.t:
                    ctx.dram_out("dbg_rows", [4, T], F32)
                for _ri, _rn in enumerate(("istd", "c0", "n1", "n0")):
                    b.dma("Q", ctx.t["dbg_rows"][_ri:_ri + 1, t0:t0 + BLK],
                          R[_rn][0:1, 0:BLK],
                          reads=[R[_rn]._res], writes=["dbg_rows"], tag="drw")
            for m in range(NC):
                wbuf = wst[m % 2]
                wv_ = wbuf[:, 0:8 * 128].rearrange("p (k o) -> p k o", k=8)
                b.dma("G", wv_, wpsrc[:, :, m * 128:(m + 1) * 128],
                      reads=["wprojT0"], writes=[wbuf._res], tag="wpld")
                b.dma("Q", x0f[:, (m % 2) * BLK:(m % 2 + 1) * BLK],
                      stream_view(x0, t0, BLK)[:, m, :],
                      reads=[x0._res], writes=[x0f._res], tag="x0c")
                p = pv[m % 2]
                for k in range(NC):
                    b.op("T", lambda t, pp=p[:, 0:BLK],
                         l=wbuf[:, k * 128:(k + 1) * 128],
                         r=attn[:, k * BLK:(k + 1) * BLK],
                         st=(k == 0), sp=(k == NC - 1):
                         t.matmul(pp, l, r, start=st, stop=sp),
                         reads=[wbuf._res, attn._res], writes=[p._res], tag="pj")
                half = (m % 2) * BLK
                b.op("V", lambda v, o=stg[:, half:half + BLK],
                     a=x0f[:, half:half + BLK], n=nb[:, 0:BLK]:
                     v.tensor_mul(o, a, n),
                     reads=[x0f._res, nb._res], writes=[stg._res], tag="e1")
                b.op("V", lambda v, o=stg[:, half:half + BLK],
                     a=stg[:, half:half + BLK], n=nb[:, BLK:2 * BLK]:
                     v.tensor_add(o, a, n),
                     reads=[stg._res, nb._res], writes=[stg._res], tag="e2")
                if DEBUG:
                    if "dbg_xa" not in ctx.t:
                        ctx.dram_out("dbg_xa", [D, T], F32)
                        ctx.dram_out("dbg_pj", [D, T], F32)
                    b.dma("Q", stream_view(ctx.t["dbg_xa"], t0, BLK)[:, m, :],
                          stg[:, half:half + BLK],
                          reads=[stg._res], writes=["dbg_xa"], tag="dxa")
                    b.op("V", lambda v, o=ctx.t["lntmp"][:, 0:BLK], i=p[:, 0:BLK]:
                         v.tensor_copy(o, i),
                         reads=[p._res], writes=["lntmp"], tag="dpj1")
                    b.dma("Q", stream_view(ctx.t["dbg_pj"], t0, BLK)[:, m, :],
                          ctx.t["lntmp"][:, 0:BLK],
                          reads=["lntmp"], writes=["dbg_pj"], tag="dpj2")
                b.op("V", lambda v, o=stg[:, half:half + BLK],
                     a=stg[:, half:half + BLK], cc=p[:, 0:BLK]:
                     v.tensor_add(o, a, cc),
                     reads=[stg._res, p._res], writes=[stg._res], tag="e3")
                b.dma("Q", stream_view(xa1, t0, BLK)[:, m, :],
                      stg[:, half:half + BLK],
                      reads=[stg._res], writes=[xa1._res], tag="xa1st")


PHASES.update({
    "A2a": Ph(emit_A2a, ["xa1", "wff1"], ["hscr"]),
    "A2b": Ph(emit_A2b, ["xa1", "x0", "hscr", "wff2"], ["x1"]),
    "F1": Ph(emit_F1, ["x1", "w1T", "w2T"], ["hscr"]),
    "F2": Ph(emit_F2, ["hscr", "h1T"], ["hscr2"]),
    "F3": Ph(emit_F3, ["hscr2", "h2T"], ["hscr"]),
    "F4": Ph(emit_F4, ["hscr", "x1", "w3T"], ["x2"]),
    "C1": Ph(emit_C1, ["x3", "cw1", "cw2", "cw3", "cw4", "gw1", "gw2", "gw3"],
             ["x4"]),
    "B1": Ph(emit_B1, ["x2", "wattnT", "wprojT1"], ["x3"],
             consts=["iden", "idenf", "mneg", "eh"]),
    "A1": Ph(emit_A1, ["x0", "wqkvfT", "wprojT0"], ["xa1"],
             consts=["iden", "idenf", "mneg", "eh", "e2", "efa", "wconv_c"]),
})
